# version 02
# baseline (speedup 1.0000x reference)
import numpy as np
import concourse.bass as bass
import concourse.mybir as mybir
from concourse.bass_utils import run_bass_kernel_spmd

F32 = mybir.dt.float32
BF16 = mybir.dt.bfloat16
AF = mybir.ActivationFunctionType
ALU = mybir.AluOpType
AX = mybir.AxisListType

B, L, DM, DI, DTR, DS, DCV = 4, 4096, 2048, 4096, 128, 16, 4
DH = DI // 2
NDC = DH // 128
KC = DM // 128
TOK = L // 2
NT_L = L // 512
NT_F = TOK // 512
TW = 512
NTW = L // TW
EPS_LN = 1e-5
EPS_MOE = 1e-8
GROUPS = [[0, 1], [2, 3], [4, 5], [6, 7]]


class Ctr:
    def __init__(self, nc, name):
        self.sem = nc.alloc_semaphore(name)
        self.n = 0

    def inc(self, ins, k=1):
        ins.then_inc(self.sem, k)
        self.n += k
        return self.n


def build_nc(dbg=(), phase_limit=99):
    nc = bass.Bass("TRN2", target_bir_lowering=False, debug=False)

    def din(name, shape, dtype=F32):
        return nc.dram_tensor(name, list(shape), dtype, kind="ExternalInput")

    def dsc(name, shape, dtype=F32):
        kind = "ExternalOutput" if name in dbg else "Internal"
        return nc.dram_tensor(name, list(shape), dtype, kind=kind)

    d_xb = din("xb", [L, DM])
    d_xtok = din("xtok", [TOK, DM])
    d_win = din("win", [32, 128, KC, 128])
    d_wxp = din("wxp", [128, NDC, 160])
    d_wdt = din("wdt", [16, 128, 128])
    d_wrt = din("wrt", [128, KC, 128])
    d_wg = din("wg", [64, 128, KC, 128])
    d_wu = din("wu", [64, 128, KC, 128])
    d_wsg = din("wsg", [16, 128, KC, 128])
    d_wsu = din("wsu", [16, 128, KC, 128])
    d_wsd = din("wsd", [16, 128, KC, 128])
    d_wo = din("wo", [16, 128, 32, 128])
    d_wdn = din("wdn", [16, 128, 8, 128])
    d_alog = din("alog", [128, NDC * DS])
    d_cw = din("cw", [128, NDC * DCV])
    d_cb = din("cb", [128, NDC])
    d_dtb = din("dtb", [128, NDC])
    d_dvec = din("dvec", [128, NDC])
    d_amp = din("amp", [128, 16])
    d_es = din("es", [128, 1])
    d_m0 = din("m0", [128, 1])
    d_m1 = din("m1", [128, 1])
    d_out = nc.dram_tensor("out", [TOK, DM], F32, kind="ExternalOutput")

    sc_xnTo = dsc("sc_xnTo", [KC, 128, TOK], BF16)
    sc_xld = dsc("sc_xld", [NDC, 128, L])
    sc_sz = dsc("sc_sz", [NDC, 128, L])
    sc_delta = dsc("sc_delta", [NDC, 128, L])
    sc_xdbl = dsc("sc_xdbl", [160, L])
    sc_xdblr = dsc("sc_xdblr", [160, L])
    sc_wmix = dsc("sc_wmix", [8, TOK])
    sc_bc = dsc("sc_bc", [8, 128, TOK], BF16)
    sc_hid = dsc("sc_hid", [16, 128, TOK], BF16)
    sc_g = dsc("sc_g", [NDC * 128, L], BF16)
    sc_gx = dsc("sc_gx", [2 * NDC * 128, L], BF16)

    sTE = Ctr(nc, "sTE")
    sSC = Ctr(nc, "sSC")
    sVE = Ctr(nc, "sVE")
    sPO = Ctr(nc, "sPO")
    sCC = Ctr(nc, "sCC")
    sP0L = Ctr(nc, "sP0L")
    sXt = [Ctr(nc, f"sXt{i}") for i in range(2)]
    sStXnTo = Ctr(nc, "sStXnTo")
    sWin = [Ctr(nc, f"sWin{i}") for i in range(2)]
    sStZ = [Ctr(nc, f"sStZ{i}") for i in range(2)]
    sStXl = [Ctr(nc, f"sStXl{i}") for i in range(2)]
    sLdXp = Ctr(nc, "sLdXp")
    sLdXb = [Ctr(nc, f"sLdXb{i}") for i in range(2)]
    sStXd = [Ctr(nc, f"sStXd{i}") for i in range(2)]
    sLdDt = [Ctr(nc, f"sLdDt{i}") for i in range(2)]
    sStDl = [Ctr(nc, f"sStDl{i}") for i in range(2)]
    sLdXn = Ctr(nc, "sLdXn")
    sLdRt = Ctr(nc, "sLdRt")
    sStWm = Ctr(nc, "sStWm")
    sLdGu = [Ctr(nc, f"sLdGu{i}") for i in range(2)]
    sStBc = Ctr(nc, "sStBc")
    sStHd = [Ctr(nc, f"sStHd{i}") for i in range(2)]
    sLdBC = Ctr(nc, "sLdBC")
    sLdSc = [Ctr(nc, f"sLdSc{i}") for i in range(2)]
    sStG = [Ctr(nc, f"sStG{i}") for i in range(2)]
    sLdG = [Ctr(nc, f"sLdG{i}") for i in range(2)]
    sLdFin = Ctr(nc, "sLdFin")
    sLdWf = [Ctr(nc, f"sLdWf{i}") for i in range(2)]
    sStOut = Ctr(nc, "sStOut")
    bankF = [Ctr(nc, f"bF{i}") for i in range(6)]
    bankU = [Ctr(nc, f"bU{i}") for i in range(6)]
    trbU, trbF = Ctr(nc, "trbU"), Ctr(nc, "trbF")
    trb_slot = [0]
    trfU, trfF = Ctr(nc, "trfU"), Ctr(nc, "trfF")

    pb = [nc.alloc_psum_tensor(f"pb{i}", [128, 512], F32) for i in range(4)]
    ptb = [nc.alloc_psum_tensor(f"ptb{i}", [128, 1024], BF16)
           for i in range(2)]
    ptf = [nc.alloc_psum_tensor(f"ptf{i}", [128, 512], F32) for i in range(2)]
    buse = [0] * 4

    sbt = nc.alloc_sbuf_tensor
    negA = sbt("negA", [128, NDC * DS], F32)
    cw = sbt("sb_cw", [128, NDC * DCV], F32)
    cb = sbt("sb_cb", [128, NDC], F32)
    dtb = sbt("sb_dtb", [128, NDC], F32)
    dvec = sbt("sb_dvec", [128, NDC], F32)
    mixes = sbt("mixes", [128, 8], BF16)
    esv = sbt("esv", [128, 1], F32)
    m0 = sbt("sb_m0", [128, 1], F32)
    m1 = sbt("sb_m1", [128, 1], F32)
    identb = sbt("identb", [128, 128], BF16)
    identf = sbt("identf", [128, 128], F32)
    state = sbt("state", [128, NDC * DS], F32)
    zz = sbt("zz", [128, 128], F32)
    ln_s = sbt("ln_s", [128, 16], F32)
    tk = sbt("tk", [128, 32], F32)
    cst = sbt("cst", [128, 4], F32)
    nc.const_aps.register(F32, EPS_LN, cst[:, 0:1])
    nc.const_aps.register(F32, 1.0, cst[:, 1:2])
    nc.const_aps.register(F32, -1.0, cst[:, 2:3])

    import contextlib

    def mkwin(names):
        es = contextlib.ExitStack()
        d = {}
        for (name, shape, dtype) in names:
            d[name] = es.enter_context(nc.sbuf_tensor(name, list(shape), dtype))
        return es, d

    blk_cm = nc.Block()
    blk = blk_cm.__enter__()

    def on(engine):
        def deco(fn):
            getattr(blk, engine)(fn)
        return deco

    bar_k = [0]

    def barrier(dma_waits=()):
        bsem = nc.alloc_semaphore(f"bar{bar_k[0]}")
        bar_k[0] += 1

        @on("sync")
        def _(sy):
            for ctr, val in dma_waits:
                sy.wait_ge(ctr.sem, val)
            sy.nop().then_inc(bsem, 1)
        for eng in ("gpsimd", "tensor", "scalar", "vector"):
            @on(eng)
            def _(e):
                e.nop().then_inc(bsem, 1)
        for eng in ("sync", "gpsimd", "tensor", "scalar", "vector"):
            @on(eng)
            def _(e):
                e.wait_ge(bsem, 5)

    def emit_finish():
        @on("sync")
        def _(sy):
            sStOut.inc(sy.dma_start(out=d_out.ap()[0:128, 0:16],
                                    in_=ln_s[:]), 16)
            sy.wait_ge(sStOut.sem, sStOut.n)
        blk_cm.__exit__(None, None, None)

    if phase_limit < 0:
        emit_finish()
        return nc

    @on("sync")
    def _(sy):
        for dd, ss in [(d_alog, negA), (d_cw, cw), (d_cb, cb), (d_dtb, dtb),
                       (d_dvec, dvec), (d_m0, m0), (d_m1, m1), (d_es, esv)]:
            sP0L.inc(sy.dma_start(out=ss[:], in_=dd.ap()), 16)
        sP0L.inc(sy.dma_start(out=tk[:, 0:16], in_=d_amp.ap()), 16)
    p0_ld = sP0L.n

    if phase_limit < 0.2:
        emit_finish()
        return nc

    @on("vector")
    def _(ve):
        ve.memset(identb[:], 0.0)
        ve.memset(identf[:], 0.0)
        ve.memset(state[:], 0.0)
        ve.memset(cst[:, 0:1], EPS_LN)
        ve.memset(cst[:, 1:2], 1.0)
        ve.memset(cst[:, 2:3], -1.0)
        sVE.inc(ve.memset(zz[:], 0.0))
    p0_ms = sVE.n

    @on("gpsimd")
    def _(gp):
        gp.wait_ge(sVE.sem, p0_ms)
        sPO.inc(gp.affine_select(
            out=identb[:], in_=identb[:], compare_op=ALU.not_equal,
            fill=1.0, base=0, pattern=[[-1, 128]], channel_multiplier=1))
        sPO.inc(gp.affine_select(
            out=identf[:], in_=identf[:], compare_op=ALU.not_equal,
            fill=1.0, base=0, pattern=[[-1, 128]], channel_multiplier=1))
    p0_po = sPO.n

    if phase_limit < 0.4:
        emit_finish()
        return nc

    @on("scalar")
    def _(sc):
        sc.wait_ge(sP0L.sem, p0_ld)
        sc.wait_ge(sVE.sem, p0_ms)
        sc.activation(out=negA[:], in_=negA[:], func=AF.Exp)
        sSC.inc(sc.activation(out=tk[:, 16:32], in_=tk[:, 0:16],
                              func=AF.Square))
    p0_sc = sSC.n

    if phase_limit < 0.6:
        emit_finish()
        return nc

    @on("vector")
    def _(ve):
        ve.wait_ge(sSC.sem, p0_sc)
        ve.tensor_scalar_mul(negA[:], negA[:], -1.0)
        sq = tk[:, 16:32].rearrange("p (k two) -> p k two", two=2)
        ve.tensor_add(tk[:, 0:8], sq[:, :, 0], sq[:, :, 1])
        ve.drain()
        ve.reduce_sum(tk[:, 8:9], tk[:, 0:8], axis=AX.X)
        ve.drain()
        ve.tensor_scalar_add(tk[:, 8:9], tk[:, 8:9], EPS_MOE)
        ve.drain()
        ve.reciprocal(tk[:, 9:10], tk[:, 8:9])
        ve.drain()
        ve.tensor_scalar(out=tk[:, 10:18], in0=tk[:, 0:8], scalar1=tk[:, 9:10],
                         scalar2=esv[:, 0:1], op0=ALU.mult, op1=ALU.mult)
        ve.drain()
        sVE.inc(ve.tensor_copy(mixes[:], tk[:, 10:18]))
    p0_ve = sVE.n

    if phase_limit < 1:
        emit_finish()
        return nc

    w1es, w1 = mkwin([
        ("xnT", [128, KC * L], BF16),
        ("xtile", [128, 2 * DM], F32),
        ("sqdump", [128, DM], F32),
        ("xnstage", [128, DM], BF16),
        ("xnstage2", [128, DM], BF16),
        ("xbld", [128, DCV - 1 + L], F32),
        ("w_in_sb", [128, 2 * KC * 128], BF16),
        ("cacc", [128, 2 * 512], F32),
        ("zsil", [128, 2 * 512], F32),
        ("xldst", [128, 2 * 512], F32),
    ])
    xnT, xtile, sqdump = w1["xnT"], w1["xtile"], w1["sqdump"]
    xnstage, xnstage2, xbld = w1["xnstage"], w1["xnstage2"], w1["xbld"]
    w_in_sb, cacc, zsil, xldst = (w1["w_in_sb"], w1["cacc"], w1["zsil"],
                                  w1["xldst"])

    norm_rec = []
    tr_last_rec = []
    var_recs = []
    xnto_st_rec = []
    gt_base = [0]

    def emit_ln(src, ntt, to_sbuf):
        base = gt_base[0]
        gt_base[0] += ntt
        for tt0 in range(ntt):
            tt = base + tt0
            slot = tt % 2

            @on("sync")
            def _(sy, tt=tt, tt0=tt0, slot=slot):
                if tt >= 2:
                    sy.wait_ge(sVE.sem, norm_rec[tt - 2])
                sXt[slot].inc(sy.dma_start(
                    out=xtile[:, slot * DM:(slot + 1) * DM],
                    in_=src.ap()[tt0 * 128:(tt0 + 1) * 128, :]), 16)
            ld_rec = (slot, sXt[slot].n)

            @on("vector")
            def _(ve, slot=slot, ld_rec=ld_rec):
                xt = xtile[:, slot * DM:(slot + 1) * DM]
                ve.wait_ge(sXt[ld_rec[0]].sem, ld_rec[1])
                ve.drain()
                ve.reduce_sum(ln_s[:, 0:1], xt, axis=AX.X)
                ve.drain()
                sVE.inc(ve.tensor_scalar_mul(ln_s[:, 1:2], ln_s[:, 0:1],
                                             1.0 / DM))
            mean_rec = sVE.n

            @on("scalar")
            def _(sc, slot=slot, ld_rec=ld_rec, tt=tt):
                xt = xtile[:, slot * DM:(slot + 1) * DM]
                sc.wait_ge(sXt[ld_rec[0]].sem, ld_rec[1])
                if tt >= 1:
                    sc.wait_ge(sVE.sem, var_recs[tt - 1])
                sc.drain()
                sSC.inc(sc.activation(out=sqdump[:], in_=xt, func=AF.Square,
                                      accum_out=ln_s[:, 2:3]))
            sq_rec = sSC.n

            @on("vector")
            def _(ve, sq_rec=sq_rec):
                ve.wait_ge(sSC.sem, sq_rec)
                ve.drain()
                ve.tensor_scalar_mul(ln_s[:, 3:4], ln_s[:, 2:3], 1.0 / DM)
                ve.tensor_mul(ln_s[:, 4:5], ln_s[:, 1:2], ln_s[:, 1:2])
                ve.drain()
                sVE.inc(ve.tensor_sub(ln_s[:, 5:6], ln_s[:, 3:4],
                                      ln_s[:, 4:5]))
            var_rec = sVE.n
            var_recs.append(var_rec)

            @on("scalar")
            def _(sc, var_rec=var_rec):
                sc.wait_ge(sVE.sem, var_rec)
                sSC.inc(sc.activation(out=ln_s[:, 6:7], in_=ln_s[:, 5:6],
                                      func=AF.Sqrt, bias=EPS_LN))
            std_rec = sSC.n

            @on("vector")
            def _(ve, slot=slot, std_rec=std_rec, tt=tt):
                xt = xtile[:, slot * DM:(slot + 1) * DM]
                ve.wait_ge(sSC.sem, std_rec)
                if tt >= 1:
                    ve.wait_ge(trbU.sem, tr_last_rec[tt - 1])
                ve.drain()
                ve.reciprocal(ln_s[:, 7:8], ln_s[:, 6:7])
                ve.drain()
                sVE.inc(ve.tensor_scalar(
                    out=xnstage[:], in0=xt,
                    scalar1=ln_s[:, 1:2], scalar2=ln_s[:, 7:8],
                    op0=ALU.subtract, op1=ALU.mult))
            norm_rec.append(sVE.n)
            xn_rec = sVE.n

            for kc2 in range(KC):
                tbank = trb_slot[0] % 2
                trb_slot[0] += 1

                @on("tensor")
                def _(te, kc2=kc2, tbank=tbank, xn_rec=xn_rec):
                    if kc2 == 0:
                        te.wait_ge(sVE.sem, xn_rec)
                        if trbU.n == 0:
                            te.wait_ge(sPO.sem, p0_po)
                    if trbU.n >= 1:
                        te.wait_ge(trbF.sem, trbU.n - 1)
                    trbU.inc(te.transpose(
                        ptb[tbank][:, 0:128],
                        xnstage[:, kc2 * 128:(kc2 + 1) * 128], identb[:]))
                tr_rec = trbU.n

                @on("scalar")
                def _(sc, kc2=kc2, tbank=tbank, tt=tt, tt0=tt0,
                      tr_rec=tr_rec):
                    sc.wait_ge(trbU.sem, tr_rec)
                    if (not to_sbuf) and kc2 == 0 and tt0 >= 1:
                        sc.wait_ge(sStXnTo.sem, xnto_st_rec[tt0 - 1])
                    if to_sbuf:
                        dst = xnT[:, kc2 * L + tt * 128:
                                  kc2 * L + (tt + 1) * 128]
                    else:
                        dst = xnstage2[:, kc2 * 128:(kc2 + 1) * 128]
                    trbF.inc(sc.activation(
                        out=dst, in_=ptb[tbank][:, 0:128],
                        func=AF.Copy))
            tr_last_rec.append(trbU.n)
            ev_rec = trbF.n

            if not to_sbuf:
                @on("sync")
                def _(sy, tt0=tt0, ev_rec=ev_rec):
                    sy.wait_ge(trbF.sem, ev_rec)
                    view = xnstage2[:].rearrange("p (k c) -> p k c", k=KC)
                    dstap = sc_xnTo.ap().transpose([1, 0, 2])[
                        :, :, tt0 * 128:(tt0 + 1) * 128]
                    sStXnTo.inc(sy.dma_start(out=dstap, in_=view), 16)
                xnto_st_rec.append(sStXnTo.n)
        return trbF.n

    p1_trbF = emit_ln(d_xb, L // 128, True)
    emit_ln(d_xtok, TOK // 128, False)
    p1b_st = sStXnTo.n

    if phase_limit < 2:
        emit_finish()
        w1es.close()
        return nc

    @on("vector")
    def _(ve):
        sVE.inc(ve.memset(xbld[:, 0:DCV - 1], 0.0))

    mc_te_rec = []
    silu_rec = {}
    xst_rec = {}
    zst_list = []
    zsil_rec = []
    for Mc in range(32):
        wslot = Mc % 2

        @on("gpsimd")
        def _(gp, Mc=Mc, wslot=wslot):
            if Mc >= 2:
                gp.wait_ge(sTE.sem, mc_te_rec[Mc - 2])
            sWin[wslot].inc(gp.dma_start(
                out=w_in_sb[:, wslot * KC * 128:(wslot + 1) * KC * 128]
                .rearrange("p (k m) -> p k m", k=KC),
                in_=d_win.ap()[Mc]), 16)
        w_rec = (wslot, sWin[wslot].n)

        for ntg in range(2):
            @on("tensor")
            def _(te, Mc=Mc, ntg=ntg, wslot=wslot, w_rec=w_rec):
                if ntg == 0:
                    te.wait_ge(sWin[w_rec[0]].sem, w_rec[1])
                    if Mc == 0:
                        te.wait_ge(trbF.sem, p1_trbF)
                for j in range(4):
                    if buse[j] >= 1:
                        te.wait_ge(bankF[j].sem, buse[j])
                for kc2 in range(KC):
                    for j in range(4):
                        nt = ntg * 4 + j
                        ins = te.matmul(
                            pb[j][:],
                            w_in_sb[:, (wslot * KC + kc2) * 128:
                                    (wslot * KC + kc2 + 1) * 128],
                            xnT[:, kc2 * L + nt * 512:
                                kc2 * L + (nt + 1) * 512],
                            start=(kc2 == 0), stop=(kc2 == KC - 1))
                        if kc2 == KC - 1:
                            bankU[j].inc(ins)
                            buse[j] += 1
                            if ntg == 1 and j == 3:
                                sTE.inc(te.drain())
            for j in range(4):
                nt = ntg * 4 + j
                u_val = bankU[j].n

                if Mc < NDC:
                    dc = Mc

                    @on("vector")
                    def _(ve, j=j, nt=nt, u_val=u_val):
                        ve.wait_ge(bankU[j].sem, u_val)
                        ins = ve.tensor_copy(
                            xbld[:, DCV - 1 + nt * 512:
                                 DCV - 1 + (nt + 1) * 512],
                            pb[j][:])
                        bankF[j].inc(ins)
                        sVE.inc(ve.drain())
                else:
                    dc = Mc - NDC
                    zidx = len(zst_list)
                    zslot = zidx % 2

                    @on("scalar")
                    def _(sc, j=j, zslot=zslot, zidx=zidx, u_val=u_val):
                        if zidx >= 2:
                            sc.wait_ge(sStZ[zslot].sem, zst_list[zidx - 2])
                        sc.wait_ge(bankU[j].sem, u_val)
                        ins = sc.activation(
                            out=zsil[:, zslot * 512:(zslot + 1) * 512],
                            in_=pb[j][:], func=AF.Silu)
                        bankF[j].inc(ins)
                        sSC.inc(sc.drain())
                    zsil_rec.append(sSC.n)

                    @on("sync")
                    def _(sy, nt=nt, dc=dc, zslot=zslot, zidx=zidx):
                        sy.wait_ge(sSC.sem, zsil_rec[zidx])
                        sStZ[zslot].inc(sy.dma_start(
                            out=sc_sz.ap()[dc, :, nt * 512:(nt + 1) * 512],
                            in_=zsil[:, zslot * 512:(zslot + 1) * 512]), 16)
                    zst_list.append(sStZ[zslot].n)
        mc_te_rec.append(sTE.n)

        if Mc < NDC:
            dc = Mc
            evac_rec = sVE.n
            for nt in range(NT_L):
                ci = dc * NT_L + nt
                cslot = ci % 2
                xslot = ci % 2

                @on("vector")
                def _(ve, dc=dc, nt=nt, ci=ci, cslot=cslot):
                    if ci >= 2:
                        ve.wait_ge(sSC.sem, silu_rec[ci - 2])
                    ve.drain()
                    dst = cacc[:, cslot * 512:(cslot + 1) * 512]
                    ve.tensor_scalar(
                        out=dst, in0=xbld[:, nt * 512:nt * 512 + 512],
                        scalar1=cw[:, dc * DCV:dc * DCV + 1], scalar2=None,
                        op0=ALU.mult)
                    last = None
                    for jj in range(1, DCV):
                        ve.drain()
                        last = ve.scalar_tensor_tensor(
                            out=dst,
                            in0=xbld[:, nt * 512 + jj:nt * 512 + 512 + jj],
                            scalar=cw[:, dc * DCV + jj:dc * DCV + jj + 1],
                            in1=dst, op0=ALU.mult, op1=ALU.add)
                    sVE.inc(last)
                cv_rec = sVE.n

                @on("scalar")
                def _(sc, dc=dc, nt=nt, ci=ci, cslot=cslot, xslot=xslot,
                      cv_rec=cv_rec):
                    sc.wait_ge(sVE.sem, cv_rec)
                    if ci >= 2:
                        sc.wait_ge(sStXl[xslot].sem, xst_rec[ci - 2])
                    sSC.inc(sc.activation(
                        out=xldst[:, xslot * 512:(xslot + 1) * 512],
                        in_=cacc[:, cslot * 512:(cslot + 1) * 512],
                        func=AF.Silu, bias=cb[:, dc:dc + 1]))
                silu_rec[ci] = sSC.n

                @on("sync")
                def _(sy, dc=dc, nt=nt, ci=ci, xslot=xslot):
                    sy.wait_ge(sSC.sem, silu_rec[ci])
                    sStXl[xslot].inc(sy.dma_start(
                        out=sc_xld.ap()[dc, :, nt * 512:(nt + 1) * 512],
                        in_=xldst[:, xslot * 512:(xslot + 1) * 512]), 16)
                xst_rec[ci] = sStXl[xslot].n

    p2_xl = (sStXl[0].n, sStXl[1].n)
    p2_z = (sStZ[0].n, sStZ[1].n)
    w1es.close()
    barrier([(sStZ[0], p2_z[0]), (sStZ[1], p2_z[1]),
             (sStXl[0], p2_xl[0]), (sStXl[1], p2_xl[1]),
             (sStXnTo, p1b_st)])

    if phase_limit < 3:
        emit_finish()
        return nc

    w2es, w2 = mkwin([
        ("xp_sb", [128, NDC * 160], BF16),
        ("xldb", [128, 2 * NDC * 512], BF16),
        ("xdbl_s", [128, 2 * 512], F32),
        ("xdbl_s2", [32, 2 * 512], F32),
        ("dt_sb", [128, 16 * 128], BF16),
        ("dtlo", [128, 2 * 512], BF16),
        ("spl", [128, 8 * 512], F32),
        ("delst", [128, 2 * 512], F32),
    ])
    xp_sb, xldb = w2["xp_sb"], w2["xldb"]
    xdbl_s, xdbl_s2 = w2["xdbl_s"], w2["xdbl_s2"]
    dt_sb, dtlo, spl, delst = w2["dt_sb"], w2["dtlo"], w2["spl"], w2["delst"]

    @on("gpsimd")
    def _(gp):
        sLdXp.inc(gp.dma_start(
            out=xp_sb[:].rearrange("p (k m) -> p k m", k=NDC),
            in_=d_wxp.ap()), 16)
        sLdXp.inc(gp.dma_start(
            out=dt_sb[:].rearrange("p (k m) -> p k m", k=16),
            in_=d_wdt.ap().transpose([1, 0, 2])), 16)
    xp_w = sLdXp.n

    p3_te_rec = []
    p3_st_rec = []
    for nt in range(NT_L):
        rslot = nt % 2

        @on("gpsimd")
        def _(gp, nt=nt, rslot=rslot):
            if nt == 0:
                gp.wait_ge(sStXl[0].sem, p2_xl[0])
                gp.wait_ge(sStXl[1].sem, p2_xl[1])
            if nt >= 2:
                gp.wait_ge(sTE.sem, p3_te_rec[nt - 2])
            for dc in range(NDC):
                sLdXb[rslot].inc(gp.dma_start(
                    out=xldb[:, (rslot * NDC + dc) * 512:
                             (rslot * NDC + dc + 1) * 512],
                    in_=sc_xld.ap()[dc, :, nt * 512:(nt + 1) * 512]), 16)
        r_rec = (rslot, sLdXb[rslot].n)

        for mci in range(2):
            j = (nt * 2 + mci) % 4
            mwid = 128 if mci == 0 else 32

            @on("tensor")
            def _(te, nt=nt, mci=mci, j=j, mwid=mwid, rslot=rslot,
                  r_rec=r_rec):
                if mci == 0:
                    te.wait_ge(sLdXb[r_rec[0]].sem, r_rec[1])
                    if nt == 0:
                        te.wait_ge(sLdXp.sem, xp_w)
                if buse[j] >= 1:
                    te.wait_ge(bankF[j].sem, buse[j])
                for kc2 in range(NDC):
                    ins = te.matmul(
                        pb[j][0:mwid, :],
                        xp_sb[:, kc2 * 160 + mci * 128:
                              kc2 * 160 + mci * 128 + mwid],
                        xldb[:, (rslot * NDC + kc2) * 512:
                             (rslot * NDC + kc2 + 1) * 512],
                        start=(kc2 == 0), stop=(kc2 == NDC - 1))
                    if kc2 == NDC - 1:
                        bankU[j].inc(ins)
                        buse[j] += 1
                        if mci == 1:
                            sTE.inc(te.drain())
            u_val = bankU[j].n

            @on("vector")
            def _(ve, j=j, mci=mci, nt=nt, u_val=u_val, rslot=rslot):
                ve.wait_ge(bankU[j].sem, u_val)
                if nt >= 2:
                    ve.wait_ge(sStXd[rslot].sem, p3_st_rec[(nt - 2) * 2 + 1])
                if mci == 0:
                    ins = ve.tensor_copy(
                        xdbl_s[:, rslot * 512:(rslot + 1) * 512], pb[j][:])
                else:
                    ins = ve.tensor_copy(
                        xdbl_s2[0:32, rslot * 512:(rslot + 1) * 512],
                        pb[j][0:32, :])
                bankF[j].inc(ins)
                sVE.inc(ve.drain())
            ev_rec = sVE.n

            @on("sync")
            def _(sy, mci=mci, nt=nt, ev_rec=ev_rec, rslot=rslot):
                sy.wait_ge(sVE.sem, ev_rec)
                if mci == 0:
                    sStXd[rslot].inc(sy.dma_start(
                        out=sc_xdbl.ap()[0:128, nt * 512:(nt + 1) * 512],
                        in_=xdbl_s[:, rslot * 512:(rslot + 1) * 512]), 16)
                else:
                    sStXd[rslot].inc(sy.dma_start(
                        out=sc_xdbl.ap()[128:160, nt * 512:(nt + 1) * 512],
                        in_=xdbl_s2[0:32, rslot * 512:(rslot + 1) * 512]), 16)
            p3_st_rec.append(sStXd[rslot].n)
        p3_te_rec.append(sTE.n)
    p3_st = (sStXd[0].n, sStXd[1].n)

    @on("gpsimd")
    def _(gp):
        gp.wait_ge(sStXd[0].sem, p3_st[0])
        gp.wait_ge(sStXd[1].sem, p3_st[1])
        sCC.inc(gp.collective_compute(
            "AllReduce", ALU.add, replica_groups=GROUPS,
            ins=[sc_xdbl.ap()], outs=[sc_xdblr.ap()]))
    cc1 = sCC.n

    if phase_limit < 4:
        emit_finish()
        w2es.close()
        return nc

    p4_te_rec = []
    p4_add_rec = {}
    p4_st_rec = {}
    for nt in range(NT_L):
        rslot = nt % 2

        @on("gpsimd")
        def _(gp, nt=nt, rslot=rslot):
            if nt == 0:
                gp.wait_ge(sCC.sem, cc1)
            if nt >= 2:
                gp.wait_ge(sTE.sem, p4_te_rec[nt - 2])
            sLdDt[rslot].inc(gp.dma_start(
                out=dtlo[:, rslot * 512:(rslot + 1) * 512],
                in_=sc_xdblr.ap()[0:128, nt * 512:(nt + 1) * 512]), 16)
        r_rec = (rslot, sLdDt[rslot].n)

        for mc in range(NDC):
            it = nt * NDC + mc
            j = it % 4
            pslot = it % 2

            @on("tensor")
            def _(te, nt=nt, mc=mc, j=j, rslot=rslot, r_rec=r_rec):
                if mc == 0:
                    te.wait_ge(sLdDt[r_rec[0]].sem, r_rec[1])
                if buse[j] >= 1:
                    te.wait_ge(bankF[j].sem, buse[j])
                ins = te.matmul(pb[j][:], dt_sb[:, mc * 128:(mc + 1) * 128],
                                dtlo[:, rslot * 512:(rslot + 1) * 512],
                                start=True, stop=True)
                bankU[j].inc(ins)
                buse[j] += 1
                if mc == NDC - 1:
                    sTE.inc(te.drain())
            u_val = bankU[j].n

            @on("scalar")
            def _(sc, j=j, mc=mc, it=it, pslot=pslot, u_val=u_val):
                sc.wait_ge(bankU[j].sem, u_val)
                if it >= 2:
                    sc.wait_ge(sVE.sem, p4_add_rec[it - 2])
                bias = dtb[:, mc:mc + 1]
                o = pslot * 2048
                sc.drain()
                sc.activation(out=spl[:, o:o + 512], in_=pb[j][:],
                              func=AF.Relu, bias=bias)
                ins = sc.activation(out=spl[:, o + 512:o + 1024],
                                    in_=pb[j][:], func=AF.Abs, bias=bias)
                bankF[j].inc(ins)
                sc.drain()
                sc.activation(out=spl[:, o + 1024:o + 1536],
                              in_=spl[:, o + 512:o + 1024],
                              func=AF.Exp, scale=-1.0)
                sc.drain()
                sSC.inc(sc.activation(out=spl[:, o + 1536:o + 2048],
                                      in_=spl[:, o + 1024:o + 1536],
                                      func=AF.Ln, bias=1.0))
            sp_rec = sSC.n

            @on("vector")
            def _(ve, sp_rec=sp_rec, pslot=pslot, it=it):
                ve.wait_ge(sSC.sem, sp_rec)
                if it >= 2:
                    ve.wait_ge(sStDl[pslot].sem, p4_st_rec[it - 2])
                o = pslot * 2048
                ins = ve.tensor_add(delst[:, pslot * 512:(pslot + 1) * 512],
                                    spl[:, o:o + 512],
                                    spl[:, o + 1536:o + 2048])
                sVE.inc(ins)
            p4_add_rec[it] = sVE.n

            @on("sync")
            def _(sy, nt=nt, mc=mc, it=it, pslot=pslot):
                sy.wait_ge(sVE.sem, p4_add_rec[it])
                sStDl[pslot].inc(sy.dma_start(
                    out=sc_delta.ap()[mc, :, nt * 512:(nt + 1) * 512],
                    in_=delst[:, pslot * 512:(pslot + 1) * 512]), 16)
            p4_st_rec[it] = sStDl[pslot].n
        p4_te_rec.append(sTE.n)
    p4_dl = (sStDl[0].n, sStDl[1].n)

    w2es.close()
    barrier([(sStDl[0], p4_dl[0]), (sStDl[1], p4_dl[1])])

    if phase_limit < 5:
        emit_finish()
        return nc

    w3es, w3 = mkwin([
        ("xnTo", [128, KC * TOK], BF16),
        ("rt_sb", [128, KC * 128], BF16),
        ("lg_s", [128, 512], F32),
        ("ltm", [128, 128], F32),
        ("emat", [128, 128], F32),
        ("maskt", [128, 128], mybir.dt.uint32),
        ("coef", [128, 128], F32),
        ("coefb", [128, 128], BF16),
        ("coT", [128, 512], BF16),
        ("wmix_s", [8, 512], F32),
        ("wg_sb", [128, 2 * KC * 128], BF16),
        ("wu_sb", [128, 2 * KC * 128], BF16),
        ("sg_t", [128, 2 * 512], BF16),
        ("su_t", [128, 2 * 512], BF16),
        ("bas_t", [128, 2 * 512], BF16),
        ("mm_t", [128, 2 * 512], BF16),
        ("wmixb", [128, 2 * TOK], BF16),
        ("bc_acc", [128, TOK], F32),
        ("bc_b16", [128, TOK], BF16),
        ("hid_t", [128, 2 * 512], BF16),
    ])
    xnTo, rt_sb, lg_s, ltm = w3["xnTo"], w3["rt_sb"], w3["lg_s"], w3["ltm"]
    emat, maskt, coef, coefb = w3["emat"], w3["maskt"], w3["coef"], w3["coefb"]
    coT, wmix_s = w3["coT"], w3["wmix_s"]
    wg_sb, wu_sb = w3["wg_sb"], w3["wu_sb"]
    sg_t, su_t, bas_t, mm_t = w3["sg_t"], w3["su_t"], w3["bas_t"], w3["mm_t"]
    wmixb, bc_acc, bc_b16, hid_t = (w3["wmixb"], w3["bc_acc"], w3["bc_b16"],
                                    w3["hid_t"])

    @on("sync")
    def _(sy):
        sy.wait_ge(sStXnTo.sem, p1b_st)
        for kc2 in range(KC):
            sLdXn.inc(sy.dma_start(out=xnTo[:, kc2 * TOK:(kc2 + 1) * TOK],
                                   in_=sc_xnTo.ap()[kc2]), 16)
    xnTo_ld = sLdXn.n

    @on("gpsimd")
    def _(gp):
        sLdRt.inc(gp.dma_start(
            out=rt_sb[:].rearrange("p (k m) -> p k m", k=KC),
            in_=d_wrt.ap()), 16)
    rt_w = sLdRt.n

    @on("vector")
    def _(ve):
        sVE.inc(ve.memset(bc_acc[:], 0.0))

    wmix_st_rec = []
    trf_cnt = [0]
    for nt in range(NT_F):
        jj = (nt * 2) % 4

        @on("tensor")
        def _(te, nt=nt):
            if nt == 0:
                te.wait_ge(sLdXn.sem, xnTo_ld)
                te.wait_ge(sLdRt.sem, rt_w)
            if buse[jj] >= 1:
                te.wait_ge(bankF[jj].sem, buse[jj])
            for kc2 in range(KC):
                ins = te.matmul(pb[jj][:], rt_sb[:, kc2 * 128:(kc2 + 1) * 128],
                                xnTo[:, kc2 * TOK + nt * 512:
                                     kc2 * TOK + (nt + 1) * 512],
                                start=(kc2 == 0), stop=(kc2 == KC - 1))
                if kc2 == KC - 1:
                    bankU[jj].inc(ins)
                    buse[jj] += 1
        u_val = bankU[jj].n

        @on("vector")
        def _(ve, u_val=u_val):
            ve.wait_ge(bankU[jj].sem, u_val)
            ins = ve.tensor_copy(lg_s[:], pb[jj][:])
            bankF[jj].inc(ins)
            sVE.inc(ve.drain())
        lg_rec = sVE.n

        for sub in range(4):
            fbank = trf_cnt[0] % 2
            trf_cnt[0] += 1

            @on("tensor")
            def _(te, sub=sub, lg_rec=lg_rec, fbank=fbank):
                if sub == 0:
                    te.wait_ge(sVE.sem, lg_rec)
                    if trfU.n == 0:
                        te.wait_ge(sPO.sem, p0_po)
                if trfU.n >= 1:
                    te.wait_ge(trfF.sem, trfU.n - 1)
                trfU.inc(te.transpose(ptf[fbank][:, 0:128],
                                      lg_s[:, sub * 128:(sub + 1) * 128],
                                      identf[:]))
            tr_rec = trfU.n

            @on("vector")
            def _(ve, sub=sub, tr_rec=tr_rec, fbank=fbank):
                ve.wait_ge(trfU.sem, tr_rec)
                ve.drain()
                ins = ve.tensor_copy(ltm[:], ptf[fbank][:, 0:128])
                trfF.inc(ins)
                ve.drain()
                ve.reduce_max(ln_s[:, 0:1], ltm[:], axis=AX.X)
                ve.drain()
                sVE.inc(ve.tensor_scalar_mul(ln_s[:, 1:2], ln_s[:, 0:1],
                                             -1.0))
            mx_rec = sVE.n

            @on("scalar")
            def _(sc, mx_rec=mx_rec):
                sc.wait_ge(sVE.sem, mx_rec)
                sSC.inc(sc.activation(out=emat[:], in_=ltm[:], func=AF.Exp,
                                      bias=ln_s[:, 1:2]))
            e_rec = sSC.n

            @on("vector")
            def _(ve, e_rec=e_rec, nt=nt, sub=sub):
                ve.wait_ge(sSC.sem, e_rec)
                if nt == 0 and sub == 0:
                    ve.wait_ge(sPO.sem, p0_po)
                ve.drain()
                ve.reduce_sum(ln_s[:, 2:3], emat[:], axis=AX.X)
                ve.max(out=tk[:, 0:8], in_=emat[:])
                ve.drain()
                ve.reduce_sum(ln_s[:, 3:4], tk[:, 0:4], axis=AX.X)
                ve.drain()
                ve.scalar_tensor_tensor(out=ln_s[:, 4:5], in0=ln_s[:, 2:3],
                                        scalar=EPS_MOE, in1=ln_s[:, 3:4],
                                        op0=ALU.mult, op1=ALU.add)
                ve.drain()
                ve.reciprocal(ln_s[:, 5:6], ln_s[:, 4:5])
                ve.tensor_scalar(out=maskt[:], in0=emat[:],
                                 scalar1=tk[:, 3:4], scalar2=None,
                                 op0=ALU.is_lt)
                ve.drain()
                ve.tensor_scalar_mul(coef[:], emat[:], ln_s[:, 5:6])
                ve.drain()
                ve.copy_predicated(coef[:], maskt[:], zz[:])
                ve.drain()
                sVE.inc(ve.tensor_copy(coefb[:], coef[:]))
            co_rec = sVE.n

            tbank5 = trb_slot[0] % 2
            trb_slot[0] += 1

            @on("tensor")
            def _(te, sub=sub, co_rec=co_rec, tbank5=tbank5):
                te.wait_ge(sVE.sem, co_rec)
                if trbU.n >= 1:
                    te.wait_ge(trbF.sem, trbU.n - 1)
                trbU.inc(te.transpose(
                    ptb[tbank5][:, 0:128], coefb[:], identb[:]))
            tb_rec = trbU.n

            @on("vector")
            def _(ve, sub=sub, tb_rec=tb_rec, tbank5=tbank5):
                ve.wait_ge(trbU.sem, tb_rec)
                ins = ve.tensor_copy(
                    coT[:, sub * 128:(sub + 1) * 128],
                    ptb[tbank5][:, 0:128])
                trbF.inc(ins)
                sVE.inc(ve.drain())
        coT_rec = sVE.n

        jw = (nt * 2 + 1) % 4

        @on("tensor")
        def _(te, nt=nt, coT_rec=coT_rec):
            te.wait_ge(sVE.sem, coT_rec)
            if nt == 0:
                te.wait_ge(sVE.sem, p0_ve)
            if buse[jw] >= 1:
                te.wait_ge(bankF[jw].sem, buse[jw])
            ins = te.matmul(pb[jw][0:8, :], mixes[:], coT[:],
                            start=True, stop=True)
            bankU[jw].inc(ins)
            buse[jw] += 1
        uw_val = bankU[jw].n

        @on("vector")
        def _(ve, uw_val=uw_val, nt=nt):
            ve.wait_ge(bankU[jw].sem, uw_val)
            if nt >= 1:
                ve.wait_ge(sStWm.sem, wmix_st_rec[nt - 1])
            ins = ve.tensor_copy(wmix_s[:], pb[jw][0:8, :])
            bankF[jw].inc(ins)
            sVE.inc(ve.drain())
        wm_rec = sVE.n

        @on("sync")
        def _(sy, nt=nt, wm_rec=wm_rec):
            sy.wait_ge(sVE.sem, wm_rec)
            sStWm.inc(sy.dma_start(
                out=sc_wmix.ap()[:, nt * 512:(nt + 1) * 512],
                in_=wmix_s[:]), 16)
        wmix_st_rec.append(sStWm.n)
    wmix_st = sStWm.n

    if phase_limit < 6:
        emit_finish()
        w3es.close()
        return nc

    gu_te_rec = []
    gu_ve_rec = []
    gu_nt_ve = []
    gu_sg_rec = {}
    bc_st_rec = []
    for ic in range(8):
        for k in range(8):
            Mc = k * 8 + ic
            it = ic * 8 + k
            wslot = it % 2

            @on("gpsimd")
            def _(gp, Mc=Mc, wslot=wslot, it=it, k=k):
                if it == 0:
                    gp.wait_ge(sStWm.sem, wmix_st)
                if it >= 2:
                    gp.wait_ge(sTE.sem, gu_te_rec[it - 2])
                    gp.wait_ge(sVE.sem, gu_ve_rec[it - 2])
                sLdGu[wslot].inc(gp.dma_start(
                    out=wg_sb[:, wslot * KC * 128:(wslot + 1) * KC * 128]
                    .rearrange("p (a m) -> p a m", a=KC),
                    in_=d_wg.ap()[Mc]), 16)
                sLdGu[wslot].inc(gp.dma_start(
                    out=wu_sb[:, wslot * KC * 128:(wslot + 1) * KC * 128]
                    .rearrange("p (a m) -> p a m", a=KC),
                    in_=d_wu.ap()[Mc]), 16)
                sLdGu[wslot].inc(gp.dma_start(
                    out=wmixb[:, wslot * TOK:(wslot + 1) * TOK],
                    in_=sc_wmix.ap()[k:k + 1, :].partition_broadcast(128)),
                    16)
            w_rec = (wslot, sLdGu[wslot].n)

            for nt in range(NT_F):
                jg = (nt % 2) * 2
                ju = jg + 1

                @on("tensor")
                def _(te, nt=nt, jg=jg, ju=ju, wslot=wslot, w_rec=w_rec):
                    if nt == 0:
                        te.wait_ge(sLdGu[w_rec[0]].sem, w_rec[1])
                    for j in (jg, ju):
                        if buse[j] >= 1:
                            te.wait_ge(bankF[j].sem, buse[j])
                    for kc2 in range(KC):
                        ins = te.matmul(
                            pb[jg][:],
                            wg_sb[:, (wslot * KC + kc2) * 128:
                                  (wslot * KC + kc2 + 1) * 128],
                            xnTo[:, kc2 * TOK + nt * 512:
                                 kc2 * TOK + (nt + 1) * 512],
                            start=(kc2 == 0), stop=(kc2 == KC - 1))
                        if kc2 == KC - 1:
                            bankU[jg].inc(ins)
                            buse[jg] += 1
                    for kc2 in range(KC):
                        ins = te.matmul(
                            pb[ju][:],
                            wu_sb[:, (wslot * KC + kc2) * 128:
                                  (wslot * KC + kc2 + 1) * 128],
                            xnTo[:, kc2 * TOK + nt * 512:
                                 kc2 * TOK + (nt + 1) * 512],
                            start=(kc2 == 0), stop=(kc2 == KC - 1))
                        if kc2 == KC - 1:
                            bankU[ju].inc(ins)
                            buse[ju] += 1
                            if nt == NT_F - 1:
                                sTE.inc(te.drain())
                ug, uu = bankU[jg].n, bankU[ju].n
                eslot = nt % 2

                @on("scalar")
                def _(sc, jg=jg, ug=ug, eslot=eslot, it=it, nt=nt):
                    sc.wait_ge(bankU[jg].sem, ug)
                    gi = it * NT_F + nt
                    if gi >= 2:
                        prev = (gi - 2)
                        sc.wait_ge(sVE.sem, gu_nt_ve[prev])
                    ins = sc.activation(
                        out=sg_t[:, eslot * 512:(eslot + 1) * 512],
                        in_=pb[jg][:], func=AF.Silu)
                    bankF[jg].inc(ins)
                    sSC.inc(sc.drain())
                gu_sg_rec[(it, nt)] = sSC.n
                sg_rec = sSC.n

                @on("vector")
                def _(ve, ju=ju, uu=uu, eslot=eslot, nt=nt, wslot=wslot,
                      sg_rec=sg_rec, it=it):
                    ve.wait_ge(bankU[ju].sem, uu)
                    ve.drain()
                    ins = ve.tensor_copy(
                        su_t[:, eslot * 512:(eslot + 1) * 512], pb[ju][:])
                    bankF[ju].inc(ins)
                    ve.wait_ge(sSC.sem, sg_rec)
                    ve.drain()
                    ve.tensor_mul(bas_t[:, eslot * 512:(eslot + 1) * 512],
                                  sg_t[:, eslot * 512:(eslot + 1) * 512],
                                  su_t[:, eslot * 512:(eslot + 1) * 512])
                    ve.drain()
                    ve.tensor_mul(mm_t[:, eslot * 512:(eslot + 1) * 512],
                                  bas_t[:, eslot * 512:(eslot + 1) * 512],
                                  wmixb[:, wslot * TOK + nt * 512:
                                        wslot * TOK + (nt + 1) * 512])
                    ve.drain()
                    ins = ve.tensor_add(
                        bc_acc[:, nt * 512:(nt + 1) * 512],
                        bc_acc[:, nt * 512:(nt + 1) * 512],
                        mm_t[:, eslot * 512:(eslot + 1) * 512])
                    sVE.inc(ins)
                gu_nt_ve.append(sVE.n)
            gu_te_rec.append(sTE.n)
            gu_ve_rec.append(sVE.n)

        @on("vector")
        def _(ve, ic=ic):
            if ic >= 1:
                ve.wait_ge(sStBc.sem, bc_st_rec[ic - 1])
            ve.drain()
            ve.tensor_copy(bc_b16[:], bc_acc[:])
            ve.drain()
            sVE.inc(ve.memset(bc_acc[:], 0.0))
        bcc_rec = sVE.n

        @on("sync")
        def _(sy, ic=ic, bcc_rec=bcc_rec):
            sy.wait_ge(sVE.sem, bcc_rec)
            sStBc.inc(sy.dma_start(out=sc_bc.ap()[ic], in_=bc_b16[:]), 16)
        bc_st_rec.append(sStBc.n)

    sh_te_rec = []
    sh_hid_rec = {}
    sh_st_rec = {}
    for Mc in range(16):
        wslot = Mc % 2

        @on("gpsimd")
        def _(gp, Mc=Mc, wslot=wslot):
            if Mc >= 2:
                gp.wait_ge(sTE.sem, sh_te_rec[Mc - 2])
            else:
                gp.wait_ge(sTE.sem, gu_te_rec[62 + Mc])
            sLdGu[wslot].inc(gp.dma_start(
                out=wg_sb[:, wslot * KC * 128:(wslot + 1) * KC * 128]
                .rearrange("p (a m) -> p a m", a=KC),
                in_=d_wsg.ap()[Mc]), 16)
            sLdGu[wslot].inc(gp.dma_start(
                out=wu_sb[:, wslot * KC * 128:(wslot + 1) * KC * 128]
                .rearrange("p (a m) -> p a m", a=KC),
                in_=d_wsu.ap()[Mc]), 16)
        w_rec = (wslot, sLdGu[wslot].n)

        for nt in range(NT_F):
            jg = (nt % 2) * 2
            ju = jg + 1
            gi = Mc * NT_F + nt

            @on("tensor")
            def _(te, nt=nt, jg=jg, ju=ju, wslot=wslot, w_rec=w_rec, Mc=Mc):
                if nt == 0:
                    te.wait_ge(sLdGu[w_rec[0]].sem, w_rec[1])
                    if Mc == 0:
                        te.wait_ge(sTE.sem, gu_te_rec[-1])
                for j in (jg, ju):
                    if buse[j] >= 1:
                        te.wait_ge(bankF[j].sem, buse[j])
                for kc2 in range(KC):
                    ins = te.matmul(
                        pb[jg][:],
                        wg_sb[:, (wslot * KC + kc2) * 128:
                              (wslot * KC + kc2 + 1) * 128],
                        xnTo[:, kc2 * TOK + nt * 512:
                             kc2 * TOK + (nt + 1) * 512],
                        start=(kc2 == 0), stop=(kc2 == KC - 1))
                    if kc2 == KC - 1:
                        bankU[jg].inc(ins)
                        buse[jg] += 1
                for kc2 in range(KC):
                    ins = te.matmul(
                        pb[ju][:],
                        wu_sb[:, (wslot * KC + kc2) * 128:
                              (wslot * KC + kc2 + 1) * 128],
                        xnTo[:, kc2 * TOK + nt * 512:
                             kc2 * TOK + (nt + 1) * 512],
                        start=(kc2 == 0), stop=(kc2 == KC - 1))
                    if kc2 == KC - 1:
                        bankU[ju].inc(ins)
                        buse[ju] += 1
                        if nt == NT_F - 1:
                            sTE.inc(te.drain())
            ug, uu = bankU[jg].n, bankU[ju].n
            eslot = nt % 2

            @on("scalar")
            def _(sc, jg=jg, ug=ug, eslot=eslot, Mc=Mc, nt=nt, gi=gi):
                sc.wait_ge(bankU[jg].sem, ug)
                if gi >= 2:
                    sc.wait_ge(sVE.sem, sh_hid_rec[gi - 2])
                else:
                    sc.wait_ge(sVE.sem, gu_ve_rec[-1])
                ins = sc.activation(
                    out=sg_t[:, eslot * 512:(eslot + 1) * 512],
                    in_=pb[jg][:], func=AF.Silu)
                bankF[jg].inc(ins)
                sSC.inc(sc.drain())
            sg_rec = sSC.n

            @on("vector")
            def _(ve, ju=ju, uu=uu, eslot=eslot, sg_rec=sg_rec, gi=gi):
                ve.wait_ge(bankU[ju].sem, uu)
                ve.drain()
                ins = ve.tensor_copy(
                    su_t[:, eslot * 512:(eslot + 1) * 512], pb[ju][:])
                bankF[ju].inc(ins)
                ve.wait_ge(sSC.sem, sg_rec)
                if gi >= 2:
                    ve.wait_ge(sStHd[eslot].sem, sh_st_rec[gi - 2])
                ve.drain()
                ins = ve.tensor_mul(
                    hid_t[:, eslot * 512:(eslot + 1) * 512],
                    sg_t[:, eslot * 512:(eslot + 1) * 512],
                    su_t[:, eslot * 512:(eslot + 1) * 512])
                sVE.inc(ins)
            sh_hid_rec[gi] = sVE.n

            @on("sync")
            def _(sy, Mc=Mc, nt=nt, gi=gi, eslot=eslot):
                sy.wait_ge(sVE.sem, sh_hid_rec[gi])
                sStHd[eslot].inc(sy.dma_start(
                    out=sc_hid.ap()[Mc, :, nt * 512:(nt + 1) * 512],
                    in_=hid_t[:, eslot * 512:(eslot + 1) * 512]), 16)
            sh_st_rec[gi] = sStHd[eslot].n
        sh_te_rec.append(sTE.n)
    p6_st_all = (sStBc.n, sStHd[0].n, sStHd[1].n)

    w3es.close()
    barrier([(sStBc, p6_st_all[0]), (sStHd[0], p6_st_all[1]),
             (sStHd[1], p6_st_all[2]), (sStWm, sStWm.n)])

    if phase_limit < 7:
        emit_finish()
        return nc

    w4es, w4 = mkwin([
        ("Bb", [128, DS * TW], BF16),
        ("Cb", [128, DS * TW], BF16),
        ("a_all", [128, 2 * DS * TW], F32),
        ("h_all", [128, DS * TW], F32),
        ("del_t", [128, 2 * TW], F32),
        ("xld_t", [128, 2 * TW], F32),
        ("sz_t", [128, 2 * TW], F32),
        ("u_t", [128, TW], F32),
        ("b_all", [128, DS * TW], F32),
        ("m_all", [128, DS * TW], BF16),
        ("y_t", [128, TW], F32),
        ("g_t", [128, 2 * TW], BF16),
    ])
    Bb, Cb, a_all, h_all = w4["Bb"], w4["Cb"], w4["a_all"], w4["h_all"]
    del_t, xld_t, sz_t = w4["del_t"], w4["xld_t"], w4["sz_t"]
    u_t, b_all, m_all = w4["u_t"], w4["b_all"], w4["m_all"]
    y_t, g_t = w4["y_t"], w4["g_t"]

    scan_rec = []
    gst_rec = []
    for tw in range(NTW):
        @on("gpsimd")
        def _(gp, tw=tw):
            if tw >= 1:
                gp.wait_ge(sVE.sem, scan_rec[tw * NDC - 1])
            for n in range(DS):
                sLdBC.inc(gp.dma_start(
                    out=Bb[:, n * TW:(n + 1) * TW],
                    in_=sc_xdblr.ap()[128 + n:129 + n, tw * TW:(tw + 1) * TW]
                    .partition_broadcast(128)), 16)
                sLdBC.inc(gp.dma_start(
                    out=Cb[:, n * TW:(n + 1) * TW],
                    in_=sc_xdblr.ap()[144 + n:145 + n, tw * TW:(tw + 1) * TW]
                    .partition_broadcast(128)), 16)
        bc_rec = sLdBC.n

        for dc in range(NDC):
            gidx = tw * NDC + dc
            slot = gidx % 2

            @on("sync")
            def _(sy, tw=tw, dc=dc, gidx=gidx, slot=slot):
                if gidx == 0:
                    sy.wait_ge(sStDl[0].sem, p4_dl[0])
                    sy.wait_ge(sStDl[1].sem, p4_dl[1])
                    sy.wait_ge(sStXl[0].sem, p2_xl[0])
                    sy.wait_ge(sStXl[1].sem, p2_xl[1])
                    sy.wait_ge(sStZ[0].sem, p2_z[0])
                    sy.wait_ge(sStZ[1].sem, p2_z[1])
                if gidx >= 2:
                    sy.wait_ge(sVE.sem, scan_rec[gidx - 2])
                sLdSc[slot].inc(sy.dma_start(
                    out=del_t[:, slot * TW:(slot + 1) * TW],
                    in_=sc_delta.ap()[dc, :, tw * TW:(tw + 1) * TW]), 16)
                sLdSc[slot].inc(sy.dma_start(
                    out=xld_t[:, slot * TW:(slot + 1) * TW],
                    in_=sc_xld.ap()[dc, :, tw * TW:(tw + 1) * TW]), 16)
                sLdSc[slot].inc(sy.dma_start(
                    out=sz_t[:, slot * TW:(slot + 1) * TW],
                    in_=sc_sz.ap()[dc, :, tw * TW:(tw + 1) * TW]), 16)
            ld_rec = (slot, sLdSc[slot].n)

            @on("scalar")
            def _(sc, dc=dc, gidx=gidx, slot=slot, ld_rec=ld_rec):
                sc.wait_ge(sLdSc[ld_rec[0]].sem, ld_rec[1])
                if gidx == 0:
                    sc.wait_ge(sVE.sem, p0_ve)
                if gidx >= 2:
                    sc.wait_ge(sVE.sem, scan_rec[gidx - 2])
                ao = slot * DS * TW
                ins = None
                for n in range(DS):
                    ins = sc.activation(
                        out=a_all[:, ao + n * TW:ao + (n + 1) * TW],
                        in_=del_t[:, slot * TW:(slot + 1) * TW],
                        func=AF.Exp,
                        scale=negA[:, dc * DS + n:dc * DS + n + 1])
                sSC.inc(ins)
            a_rec = sSC.n

            @on("vector")
            def _(ve, tw=tw, dc=dc, gidx=gidx, slot=slot, ld_rec=ld_rec,
                  a_rec=a_rec, bc_rec=bc_rec):
                ve.wait_ge(sLdSc[ld_rec[0]].sem, ld_rec[1])
                ve.wait_ge(sLdBC.sem, bc_rec)
                ve.wait_ge(sSC.sem, a_rec)
                if gidx >= 2:
                    ve.wait_ge(sStG[slot].sem, gst_rec[gidx - 2])
                dslc = del_t[:, slot * TW:(slot + 1) * TW]
                xslc = xld_t[:, slot * TW:(slot + 1) * TW]
                ao = slot * DS * TW
                ve.drain()
                ve.tensor_mul(u_t[:], dslc, xslc)
                ve.drain()
                for n in range(DS):
                    ve.tensor_mul(b_all[:, n * TW:(n + 1) * TW], u_t[:],
                                  Bb[:, n * TW:(n + 1) * TW])
                ve.drain()
                for n in range(DS):
                    ve.tensor_tensor_scan(
                        out=h_all[:, n * TW:(n + 1) * TW],
                        data0=a_all[:, ao + n * TW:ao + (n + 1) * TW],
                        data1=b_all[:, n * TW:(n + 1) * TW],
                        initial=state[:, dc * DS + n:dc * DS + n + 1],
                        op0=ALU.mult, op1=ALU.add)
                ve.drain()
                for n in range(DS):
                    ve.tensor_mul(m_all[:, n * TW:(n + 1) * TW],
                                  h_all[:, n * TW:(n + 1) * TW],
                                  Cb[:, n * TW:(n + 1) * TW])
                hv = h_all[:].rearrange("p (n t) -> p n t", n=DS)
                ve.tensor_copy(state[:, dc * DS:(dc + 1) * DS],
                               hv[:, :, TW - 1])
                ve.drain()
                mv = m_all[:].rearrange("p (n t) -> p t n", n=DS)
                ve.reduce_sum(y_t[:], mv, axis=AX.X)
                ve.drain()
                ve.scalar_tensor_tensor(out=y_t[:], in0=xslc,
                                        scalar=dvec[:, dc:dc + 1], in1=y_t[:],
                                        op0=ALU.mult, op1=ALU.add)
                ve.drain()
                ins = ve.tensor_mul(g_t[:, slot * TW:(slot + 1) * TW], y_t[:],
                                    sz_t[:, slot * TW:(slot + 1) * TW])
                sVE.inc(ins)
            scan_rec.append(sVE.n)

            @on("sync")
            def _(sy, tw=tw, dc=dc, gidx=gidx, slot=slot):
                sy.wait_ge(sVE.sem, scan_rec[gidx])
                sStG[slot].inc(sy.dma_start(
                    out=sc_g.ap()[dc * 128:(dc + 1) * 128,
                                  tw * TW:(tw + 1) * TW],
                    in_=g_t[:, slot * TW:(slot + 1) * TW]), 16)
            gst_rec.append(sStG[slot].n)
    p7_st = (sStG[0].n, sStG[1].n)

    if phase_limit < 8:
        emit_finish()
        w4es.close()
        return nc

    @on("gpsimd")
    def _(gp):
        gp.wait_ge(sStG[0].sem, p7_st[0])
        gp.wait_ge(sStG[1].sem, p7_st[1])
        sCC.inc(gp.collective_compute(
            "AllGather", ALU.bypass, replica_groups=GROUPS,
            ins=[sc_g.ap()], outs=[sc_gx.ap()]))
    cc2 = sCC.n

    w4es.close()
    barrier()

    if phase_limit < 9:
        emit_finish()
        return nc

    w5es, w5 = mkwin([
        ("gkc", [128, 32 * 512], BF16),
        ("gtmp", [128, 4 * 512], BF16),
        ("bckc", [128, 8 * 512], BF16),
        ("hidkc", [128, 16 * 512], BF16),
        ("wf_sb", [128, 2 * 56 * 128], BF16),
        ("sfin", [128, 2 * 512], BF16),
        ("xfin", [128, 4 * DM], F32),
        ("ostage", [128, 4 * DM], F32),
    ])
    gkc, gtmp, bckc, hidkc = w5["gkc"], w5["gtmp"], w5["bckc"], w5["hidkc"]
    wf_sb, sfin, xfin, ostage = (w5["wf_sb"], w5["sfin"], w5["xfin"],
                                 w5["ostage"])

    fin_te_rec = []
    fin_ve_rec = []
    sfin_tr_rec = {}
    gcomb_rec = {}
    ost_rec = []
    for nt in range(NT_F):
        for kc2 in range(32):
            pslot = kc2 % 2

            @on("sync")
            def _(sy, nt=nt, kc2=kc2, pslot=pslot):
                if nt == 0 and kc2 == 0:
                    sy.wait_ge(sCC.sem, cc2)
                if kc2 >= 2:
                    sy.wait_ge(sVE.sem, gcomb_rec[(nt, kc2 - 2)])
                elif kc2 < 2 and nt >= 1:
                    sy.wait_ge(sVE.sem, gcomb_rec[(nt - 1, 30 + kc2)])
                r0 = kc2 * 128
                sLdG[pslot].inc(sy.dma_start(
                    out=gtmp[:, pslot * 1024:pslot * 1024 + 512],
                    in_=sc_gx.ap()[r0:r0 + 128, nt * 512:(nt + 1) * 512]),
                    16)
                sLdG[pslot].inc(sy.dma_start(
                    out=gtmp[:, pslot * 1024 + 512:pslot * 1024 + 1024],
                    in_=sc_gx.ap()[r0:r0 + 128,
                                   TOK + nt * 512:TOK + (nt + 1) * 512]), 16)
            gld_rec = (pslot, sLdG[pslot].n)

            @on("vector")
            def _(ve, nt=nt, kc2=kc2, pslot=pslot, gld_rec=gld_rec):
                ve.wait_ge(sLdG[gld_rec[0]].sem, gld_rec[1])
                if nt >= 1 and kc2 == 0:
                    ve.wait_ge(sTE.sem, fin_te_rec[nt * 16 - 1])
                ve.drain()
                ve.tensor_scalar_mul(
                    gkc[:, kc2 * 512:(kc2 + 1) * 512],
                    gtmp[:, pslot * 1024:pslot * 1024 + 512], m0[:, 0:1])
                ve.drain()
                ins = ve.scalar_tensor_tensor(
                    out=gkc[:, kc2 * 512:(kc2 + 1) * 512],
                    in0=gtmp[:, pslot * 1024 + 512:pslot * 1024 + 1024],
                    scalar=m1[:, 0:1],
                    in1=gkc[:, kc2 * 512:(kc2 + 1) * 512],
                    op0=ALU.mult, op1=ALU.add)
                sVE.inc(ins)
            gcomb_rec[(nt, kc2)] = sVE.n

        @on("sync")
        def _(sy, nt=nt):
            if nt == 0:
                sy.wait_ge(sStBc.sem, p6_st_all[0])
                sy.wait_ge(sStHd[0].sem, p6_st_all[1])
                sy.wait_ge(sStHd[1].sem, p6_st_all[2])
            if nt >= 1:
                sy.wait_ge(sTE.sem, fin_te_rec[nt * 16 - 1])
                sy.wait_ge(sVE.sem, fin_ve_rec[nt - 1])
            for kk in range(8):
                sLdFin.inc(sy.dma_start(
                    out=bckc[:, kk * 512:(kk + 1) * 512],
                    in_=sc_bc.ap()[kk, :, nt * 512:(nt + 1) * 512]), 16)
            for kk in range(16):
                sLdFin.inc(sy.dma_start(
                    out=hidkc[:, kk * 512:(kk + 1) * 512],
                    in_=sc_hid.ap()[kk, :, nt * 512:(nt + 1) * 512]), 16)
            for sub in range(4):
                sLdFin.inc(sy.dma_start(
                    out=xfin[:, sub * DM:(sub + 1) * DM],
                    in_=d_xtok.ap()[nt * 512 + sub * 128:
                                    nt * 512 + (sub + 1) * 128, :]), 16)
        rhs_rec = sLdFin.n

        for mc in range(16):
            it = nt * 16 + mc
            wslot = it % 2
            j = it % 4

            @on("gpsimd")
            def _(gp, mc=mc, wslot=wslot, it=it):
                if it >= 2:
                    gp.wait_ge(sTE.sem, fin_te_rec[it - 2])
                base = wslot * 56 * 128
                sLdWf[wslot].inc(gp.dma_start(
                    out=wf_sb[:, base:base + 32 * 128]
                    .rearrange("p (a m) -> p a m", a=32),
                    in_=d_wo.ap()[mc]), 16)
                sLdWf[wslot].inc(gp.dma_start(
                    out=wf_sb[:, base + 32 * 128:base + 40 * 128]
                    .rearrange("p (a m) -> p a m", a=8),
                    in_=d_wdn.ap()[mc]), 16)
                sLdWf[wslot].inc(gp.dma_start(
                    out=wf_sb[:, base + 40 * 128:base + 56 * 128]
                    .rearrange("p (a m) -> p a m", a=KC),
                    in_=d_wsd.ap()[mc]), 16)
            wf_rec = (wslot, sLdWf[wslot].n)

            @on("tensor")
            def _(te, nt=nt, mc=mc, j=j, wslot=wslot, wf_rec=wf_rec,
                  rhs_rec=rhs_rec, it=it):
                te.wait_ge(sLdWf[wf_rec[0]].sem, wf_rec[1])
                if mc == 0:
                    te.wait_ge(sVE.sem, gcomb_rec[(nt, 31)])
                    te.wait_ge(sLdFin.sem, rhs_rec)
                if buse[j] >= 1:
                    te.wait_ge(bankF[j].sem, buse[j])
                base = wslot * 56 * 128
                for kc2 in range(32):
                    te.matmul(pb[j][:],
                              wf_sb[:, base + kc2 * 128:
                                    base + (kc2 + 1) * 128],
                              gkc[:, kc2 * 512:(kc2 + 1) * 512],
                              start=(kc2 == 0), stop=False)
                for kk in range(8):
                    te.matmul(pb[j][:],
                              wf_sb[:, base + (32 + kk) * 128:
                                    base + (33 + kk) * 128],
                              bckc[:, kk * 512:(kk + 1) * 512],
                              start=False, stop=False)
                for kk in range(16):
                    ins = te.matmul(pb[j][:],
                                    wf_sb[:, base + (40 + kk) * 128:
                                          base + (41 + kk) * 128],
                                    hidkc[:, kk * 512:(kk + 1) * 512],
                                    start=False, stop=(kk == 15))
                    if kk == 15:
                        bankU[j].inc(ins)
                        buse[j] += 1
                        sTE.inc(te.drain())
            fin_te_rec.append(sTE.n)
            u_val = bankU[j].n
            eslot = it % 2

            @on("scalar")
            def _(sc, j=j, u_val=u_val, eslot=eslot, it=it):
                sc.wait_ge(bankU[j].sem, u_val)
                if it >= 2:
                    sc.wait_ge(trbU.sem, sfin_tr_rec[it - 2])
                ins = sc.activation(
                    out=sfin[:, eslot * 512:(eslot + 1) * 512],
                    in_=pb[j][:], func=AF.Copy)
                bankF[j].inc(ins)
                sSC.inc(sc.drain())
            sf_rec = sSC.n

            for sub in range(4):
                tbank9 = trb_slot[0] % 2
                trb_slot[0] += 1

                @on("tensor")
                def _(te, sub=sub, eslot=eslot, sf_rec=sf_rec, tbank9=tbank9):
                    if sub == 0:
                        te.wait_ge(sSC.sem, sf_rec)
                    if trbU.n >= 1:
                        te.wait_ge(trbF.sem, trbU.n - 1)
                    trbU.inc(te.transpose(
                        ptb[tbank9][:, 0:128],
                        sfin[:, eslot * 512 + sub * 128:
                             eslot * 512 + (sub + 1) * 128], identb[:]))
                tb_rec = trbU.n

                @on("vector")
                def _(ve, mc=mc, sub=sub, tb_rec=tb_rec, nt=nt,
                      tbank9=tbank9):
                    ve.wait_ge(trbU.sem, tb_rec)
                    if nt >= 1 and mc == 0 and sub == 0:
                        ve.wait_ge(sStOut.sem, ost_rec[nt - 1])
                    ins = ve.tensor_add(
                        ostage[:, sub * DM + mc * 128:
                               sub * DM + (mc + 1) * 128],
                        ptb[tbank9][:, 0:128],
                        xfin[:, sub * DM + mc * 128:sub * DM + (mc + 1) * 128])
                    trbF.inc(ins)
                    sVE.inc(ve.drain())
            sfin_tr_rec[it] = trbU.n
        fin_ve_rec.append(sVE.n)

        @on("sync")
        def _(sy, nt=nt):
            sy.wait_ge(sVE.sem, fin_ve_rec[nt])
            for sub in range(4):
                sStOut.inc(sy.dma_start(
                    out=d_out.ap()[nt * 512 + sub * 128:
                                   nt * 512 + (sub + 1) * 128, :],
                    in_=ostage[:, sub * DM:(sub + 1) * DM]), 16)
        ost_rec.append(sStOut.n)

    @on("sync")
    def _(sy):
        sy.wait_ge(sStOut.sem, sStOut.n)

    blk_cm.__exit__(None, None, None)
    w5es.close()
    return nc


def _stage(inputs, c):
    b, h = c // 2, c % 2
    f = np.ascontiguousarray

    def tile_k_m(w):
        Kd, Md = w.shape
        return f(w.reshape(Kd // 128, 128, Md // 128, 128)
                 .transpose(2, 1, 0, 3))

    x = inputs["L0_attn_hidden_states"]
    ipw = inputs["L0_attn_in_proj_w"]
    w_sel = np.concatenate([ipw[h * DH:(h + 1) * DH],
                            ipw[DI + h * DH:DI + (h + 1) * DH]], 0)
    win = tile_k_m(f(w_sel.T))
    wxp = f(inputs["L0_attn_x_proj_w"][:, h * DH:(h + 1) * DH].T
            .reshape(NDC, 128, 160).transpose(1, 0, 2))
    wdt = f(inputs["L0_attn_dt_proj_w"][h * DH:(h + 1) * DH].T
            .reshape(128, 16, 128).transpose(1, 0, 2))
    wrt = f(inputs["L0_ffn_w_router"].reshape(KC, 128, 128))
    wrt = f(wrt.transpose(1, 0, 2))
    wg = tile_k_m(inputs["L0_ffn_w_gate"])
    wu = tile_k_m(inputs["L0_ffn_w_up"])
    wsg = tile_k_m(inputs["L0_ffn_shared_w_gate"])
    wsu = tile_k_m(inputs["L0_ffn_shared_w_up"])
    wsd = tile_k_m(inputs["L0_ffn_shared_w_down"])
    wo = tile_k_m(f(inputs["L0_attn_out_proj_w"].T))
    wdn = tile_k_m(inputs["L0_ffn_w_down"])

    def vec_tile(v, width):
        return f(v.reshape(NDC, 128, width).transpose(1, 0, 2)
                 .reshape(128, NDC * width))

    alog = vec_tile(inputs["L0_attn_A_log"][h * DH:(h + 1) * DH], DS)
    cwv = vec_tile(inputs["L0_attn_conv_w"][h * DH:(h + 1) * DH], DCV)
    cbv = vec_tile(inputs["L0_attn_conv_b"][h * DH:(h + 1) * DH, None], 1)
    dtbv = vec_tile(inputs["L0_attn_dt_proj_b"][h * DH:(h + 1) * DH, None], 1)
    dvv = vec_tile(inputs["L0_attn_D"][h * DH:(h + 1) * DH, None], 1)
    amp = f(inputs["L0_ffn_amplitudes"].reshape(128, 16))
    es = f(inputs["L0_ffn_expert_scale"].reshape(128, 1))
    m0v = np.full((128, 1), 1.0 - h, np.float32)
    m1v = np.full((128, 1), float(h), np.float32)

    return dict(
        xb=f(x[b]), xtok=f(x[b, h * TOK:(h + 1) * TOK]),
        win=win, wxp=wxp, wdt=wdt, wrt=wrt, wg=wg, wu=wu, wsg=wsg, wsu=wsu,
        wsd=wsd, wo=wo, wdn=wdn, alog=alog, cw=cwv, cb=cbv, dtb=dtbv,
        dvec=dvv, amp=amp, es=es, m0=m0v, m1=m1v,
    )


def kernel(**inputs):
    inputs = {k: np.asarray(v) for k, v in inputs.items()}
    nc = build_nc()
    in_maps = [_stage(inputs, c) for c in range(8)]
    res = run_bass_kernel_spmd(nc, in_maps, core_ids=list(range(8)))
    out = np.empty((B, L, DM), np.float32)
    for c in range(8):
        b, h = c // 2, c % 2
        out[b, h * TOK:(h + 1) * TOK] = res.results[c]["out"]
    return out


# version 04
# speedup vs baseline: 1.0779x; 1.0779x over previous
import numpy as np
import concourse.bass as bass
import concourse.mybir as mybir
from concourse.bass_utils import run_bass_kernel_spmd

F32 = mybir.dt.float32
BF16 = mybir.dt.bfloat16
AF = mybir.ActivationFunctionType
ALU = mybir.AluOpType
AX = mybir.AxisListType

B, L, DM, DI, DTR, DS, DCV = 4, 4096, 2048, 4096, 128, 16, 4
DH = DI // 2
NDC = DH // 128
KC = DM // 128
TOK = L // 2
NT_L = L // 512
NT_F = TOK // 512
TW = 512
NTW = L // TW
EPS_LN = 1e-5
EPS_MOE = 1e-8
GROUPS = [[0, 1], [2, 3], [4, 5], [6, 7]]


class Ctr:
    def __init__(self, nc, name):
        self.sem = nc.alloc_semaphore(name)
        self.n = 0

    def inc(self, ins, k=1):
        ins.then_inc(self.sem, k)
        self.n += k
        return self.n


def build_nc(dbg=(), phase_limit=99):
    nc = bass.Bass("TRN2", target_bir_lowering=False, debug=False)

    def din(name, shape, dtype=F32):
        return nc.dram_tensor(name, list(shape), dtype, kind="ExternalInput")

    def dsc(name, shape, dtype=F32):
        kind = "ExternalOutput" if name in dbg else "Internal"
        return nc.dram_tensor(name, list(shape), dtype, kind=kind)

    d_xb = din("xb", [L, DM])
    d_xtok = din("xtok", [TOK, DM])
    d_win = din("win", [32, 128, KC, 128])
    d_wxp = din("wxp", [128, NDC, 160])
    d_wdt = din("wdt", [16, 128, 128])
    d_wrt = din("wrt", [128, KC, 128])
    d_wg = din("wg", [64, 128, KC, 128])
    d_wu = din("wu", [64, 128, KC, 128])
    d_wsg = din("wsg", [16, 128, KC, 128])
    d_wsu = din("wsu", [16, 128, KC, 128])
    d_wsd = din("wsd", [16, 128, KC, 128])
    d_wo = din("wo", [16, 128, 32, 128])
    d_wdn = din("wdn", [16, 128, 8, 128])
    d_alog = din("alog", [128, NDC * DS])
    d_cw = din("cw", [128, NDC * DCV])
    d_cb = din("cb", [128, NDC])
    d_dtb = din("dtb", [128, NDC])
    d_dvec = din("dvec", [128, NDC])
    d_amp = din("amp", [128, 16])
    d_es = din("es", [128, 1])
    d_m0 = din("m0", [128, 1])
    d_m1 = din("m1", [128, 1])
    d_out = nc.dram_tensor("out", [TOK, DM], F32, kind="ExternalOutput")

    sc_xnTo = dsc("sc_xnTo", [KC, 128, TOK], BF16)
    sc_xld = dsc("sc_xld", [NDC, 128, L])
    sc_sz = dsc("sc_sz", [NDC, 128, L])
    sc_delta = dsc("sc_delta", [NDC, 128, L])
    sc_xdbl = dsc("sc_xdbl", [160, L])
    sc_xdblr = dsc("sc_xdblr", [160, L])
    sc_wmix = dsc("sc_wmix", [8, TOK])
    sc_bc = dsc("sc_bc", [8, 128, TOK], BF16)
    sc_hid = dsc("sc_hid", [16, 128, TOK], BF16)
    sc_g = dsc("sc_g", [NDC * 128, L], BF16)
    sc_gx = dsc("sc_gx", [2 * NDC * 128, L], BF16)

    sTE = Ctr(nc, "sTE")
    sSC = Ctr(nc, "sSC")
    sVE = Ctr(nc, "sVE")
    sPO = Ctr(nc, "sPO")
    sCC = Ctr(nc, "sCC")
    sP0L = Ctr(nc, "sP0L")
    sXt = [Ctr(nc, f"sXt{i}") for i in range(2)]
    sStXnTo = Ctr(nc, "sStXnTo")
    sWin = [Ctr(nc, f"sWin{i}") for i in range(2)]
    sStZ = [Ctr(nc, f"sStZ{i}") for i in range(2)]
    sStXl = [Ctr(nc, f"sStXl{i}") for i in range(2)]
    sLdXp = Ctr(nc, "sLdXp")
    sLdXb = [Ctr(nc, f"sLdXb{i}") for i in range(2)]
    sStXd = [Ctr(nc, f"sStXd{i}") for i in range(2)]
    sLdDt = [Ctr(nc, f"sLdDt{i}") for i in range(2)]
    sStDl = [Ctr(nc, f"sStDl{i}") for i in range(2)]
    sLdXn = Ctr(nc, "sLdXn")
    sLdRt = Ctr(nc, "sLdRt")
    sStWm = Ctr(nc, "sStWm")
    sLdGu = [Ctr(nc, f"sLdGu{i}") for i in range(2)]
    sStBc = Ctr(nc, "sStBc")
    sStHd = [Ctr(nc, f"sStHd{i}") for i in range(2)]
    sLdBC = Ctr(nc, "sLdBC")
    sLdSc = [Ctr(nc, f"sLdSc{i}") for i in range(2)]
    sStG = [Ctr(nc, f"sStG{i}") for i in range(2)]
    sLdG = [Ctr(nc, f"sLdG{i}") for i in range(2)]
    sLdFin = Ctr(nc, "sLdFin")
    sLdWf = [Ctr(nc, f"sLdWf{i}") for i in range(2)]
    sStOut = Ctr(nc, "sStOut")
    bankF = [Ctr(nc, f"bF{i}") for i in range(6)]
    bankU = [Ctr(nc, f"bU{i}") for i in range(6)]
    trbU, trbF = Ctr(nc, "trbU"), Ctr(nc, "trbF")
    trb_slot = [0]
    trfU, trfF = Ctr(nc, "trfU"), Ctr(nc, "trfF")

    pb = [nc.alloc_psum_tensor(f"pb{i}", [128, 512], F32) for i in range(4)]
    ptb = [nc.alloc_psum_tensor(f"ptb{i}", [128, 1024], BF16)
           for i in range(2)]
    ptf = [nc.alloc_psum_tensor(f"ptf{i}", [128, 512], F32) for i in range(2)]
    buse = [0] * 4

    sbt = nc.alloc_sbuf_tensor
    negA = sbt("negA", [128, NDC * DS], F32)
    cw = sbt("sb_cw", [128, NDC * DCV], F32)
    cb = sbt("sb_cb", [128, NDC], F32)
    dtb = sbt("sb_dtb", [128, NDC], F32)
    dvec = sbt("sb_dvec", [128, NDC], F32)
    mixes = sbt("mixes", [128, 8], BF16)
    esv = sbt("esv", [128, 1], F32)
    m0 = sbt("sb_m0", [128, 1], F32)
    m1 = sbt("sb_m1", [128, 1], F32)
    identb = sbt("identb", [128, 128], BF16)
    identf = sbt("identf", [128, 128], F32)
    state = sbt("state", [128, NDC * DS], F32)
    zz = sbt("zz", [128, 128], F32)
    ln_s = sbt("ln_s", [128, 16], F32)
    tk = sbt("tk", [128, 32], F32)
    cst = sbt("cst", [128, 4], F32)
    nc.const_aps.register(F32, EPS_LN, cst[:, 0:1])
    nc.const_aps.register(F32, 1.0, cst[:, 1:2])
    nc.const_aps.register(F32, -1.0, cst[:, 2:3])

    import contextlib

    def mkwin(names):
        es = contextlib.ExitStack()
        d = {}
        for (name, shape, dtype) in names:
            d[name] = es.enter_context(nc.sbuf_tensor(name, list(shape), dtype))
        return es, d

    blk_cm = nc.Block()
    blk = blk_cm.__enter__()

    def on(engine):
        def deco(fn):
            getattr(blk, engine)(fn)
        return deco

    bar_k = [0]

    def barrier(dma_waits=()):
        bsem = nc.alloc_semaphore(f"bar{bar_k[0]}")
        bar_k[0] += 1

        @on("sync")
        def _(sy):
            for ctr, val in dma_waits:
                sy.wait_ge(ctr.sem, val)
            sy.nop().then_inc(bsem, 1)
        for eng in ("gpsimd", "tensor", "scalar", "vector"):
            @on(eng)
            def _(e):
                e.nop().then_inc(bsem, 1)
        for eng in ("sync", "gpsimd", "tensor", "scalar", "vector"):
            @on(eng)
            def _(e):
                e.wait_ge(bsem, 5)

    def emit_finish():
        @on("sync")
        def _(sy):
            sStOut.inc(sy.dma_start(out=d_out.ap()[0:128, 0:16],
                                    in_=ln_s[:]), 16)
            sy.wait_ge(sStOut.sem, sStOut.n)
        blk_cm.__exit__(None, None, None)

    if phase_limit < 0:
        emit_finish()
        return nc

    @on("sync")
    def _(sy):
        for dd, ss in [(d_alog, negA), (d_cw, cw), (d_cb, cb), (d_dtb, dtb),
                       (d_dvec, dvec), (d_m0, m0), (d_m1, m1), (d_es, esv)]:
            sP0L.inc(sy.dma_start(out=ss[:], in_=dd.ap()), 16)
        sP0L.inc(sy.dma_start(out=tk[:, 0:16], in_=d_amp.ap()), 16)
    p0_ld = sP0L.n

    if phase_limit < 0.2:
        emit_finish()
        return nc

    @on("vector")
    def _(ve):
        ve.memset(identb[:], 0.0)
        ve.memset(identf[:], 0.0)
        ve.memset(state[:], 0.0)
        ve.memset(cst[:, 0:1], EPS_LN)
        ve.memset(cst[:, 1:2], 1.0)
        ve.memset(cst[:, 2:3], -1.0)
        sVE.inc(ve.memset(zz[:], 0.0))
    p0_ms = sVE.n

    @on("gpsimd")
    def _(gp):
        gp.wait_ge(sVE.sem, p0_ms)
        sPO.inc(gp.affine_select(
            out=identb[:], in_=identb[:], compare_op=ALU.not_equal,
            fill=1.0, base=0, pattern=[[-1, 128]], channel_multiplier=1))
        sPO.inc(gp.affine_select(
            out=identf[:], in_=identf[:], compare_op=ALU.not_equal,
            fill=1.0, base=0, pattern=[[-1, 128]], channel_multiplier=1))
    p0_po = sPO.n

    if phase_limit < 0.4:
        emit_finish()
        return nc

    @on("scalar")
    def _(sc):
        sc.wait_ge(sP0L.sem, p0_ld)
        sc.wait_ge(sVE.sem, p0_ms)
        sc.activation(out=negA[:], in_=negA[:], func=AF.Exp)
        sSC.inc(sc.activation(out=tk[:, 16:32], in_=tk[:, 0:16],
                              func=AF.Square))
    p0_sc = sSC.n

    if phase_limit < 0.6:
        emit_finish()
        return nc

    @on("vector")
    def _(ve):
        ve.wait_ge(sSC.sem, p0_sc)
        ve.tensor_scalar_mul(negA[:], negA[:], -1.0)
        sq = tk[:, 16:32].rearrange("p (k two) -> p k two", two=2)
        ve.tensor_add(tk[:, 0:8], sq[:, :, 0], sq[:, :, 1])
        ve.drain()
        ve.reduce_sum(tk[:, 8:9], tk[:, 0:8], axis=AX.X)
        ve.drain()
        ve.tensor_scalar_add(tk[:, 8:9], tk[:, 8:9], EPS_MOE)
        ve.drain()
        ve.reciprocal(tk[:, 9:10], tk[:, 8:9])
        ve.drain()
        ve.tensor_scalar(out=tk[:, 10:18], in0=tk[:, 0:8], scalar1=tk[:, 9:10],
                         scalar2=esv[:, 0:1], op0=ALU.mult, op1=ALU.mult)
        ve.drain()
        sVE.inc(ve.tensor_copy(mixes[:], tk[:, 10:18]))
    p0_ve = sVE.n

    if phase_limit < 1:
        emit_finish()
        return nc

    w1es, w1 = mkwin([
        ("xnT", [128, KC * L], BF16),
        ("xtile", [128, 2 * DM], F32),
        ("sqdump", [128, DM], F32),
        ("xnstage", [128, DM], BF16),
        ("xnstage2", [128, DM], BF16),
        ("xbld", [128, DCV - 1 + L], F32),
        ("w_in_sb", [128, 2 * KC * 128], BF16),
        ("cacc", [128, 2 * 512], F32),
        ("zsil", [128, 2 * 512], F32),
        ("xldst", [128, 2 * 512], F32),
    ])
    xnT, xtile, sqdump = w1["xnT"], w1["xtile"], w1["sqdump"]
    xnstage, xnstage2, xbld = w1["xnstage"], w1["xnstage2"], w1["xbld"]
    w_in_sb, cacc, zsil, xldst = (w1["w_in_sb"], w1["cacc"], w1["zsil"],
                                  w1["xldst"])

    norm_rec = []
    tr_last_rec = []
    var_recs = []
    xnto_st_rec = []
    gt_base = [0]

    def emit_ln(src, ntt, to_sbuf):
        base = gt_base[0]
        gt_base[0] += ntt
        for tt0 in range(ntt):
            tt = base + tt0
            slot = tt % 2

            @on("sync")
            def _(sy, tt=tt, tt0=tt0, slot=slot):
                if tt >= 2:
                    sy.wait_ge(sVE.sem, norm_rec[tt - 2])
                sXt[slot].inc(sy.dma_start(
                    out=xtile[:, slot * DM:(slot + 1) * DM],
                    in_=src.ap()[tt0 * 128:(tt0 + 1) * 128, :]), 16)
            ld_rec = (slot, sXt[slot].n)

            @on("vector")
            def _(ve, slot=slot, ld_rec=ld_rec):
                xt = xtile[:, slot * DM:(slot + 1) * DM]
                ve.wait_ge(sXt[ld_rec[0]].sem, ld_rec[1])
                ve.drain()
                ve.reduce_sum(ln_s[:, 0:1], xt, axis=AX.X)
                ve.drain()
                sVE.inc(ve.tensor_scalar_mul(ln_s[:, 1:2], ln_s[:, 0:1],
                                             1.0 / DM))
            mean_rec = sVE.n

            @on("scalar")
            def _(sc, slot=slot, ld_rec=ld_rec, tt=tt):
                xt = xtile[:, slot * DM:(slot + 1) * DM]
                sc.wait_ge(sXt[ld_rec[0]].sem, ld_rec[1])
                if tt >= 1:
                    sc.wait_ge(sVE.sem, var_recs[tt - 1])
                sc.drain()
                sSC.inc(sc.activation(out=sqdump[:], in_=xt, func=AF.Square,
                                      accum_out=ln_s[:, 2:3]))
            sq_rec = sSC.n

            @on("vector")
            def _(ve, sq_rec=sq_rec):
                ve.wait_ge(sSC.sem, sq_rec)
                ve.drain()
                ve.tensor_scalar_mul(ln_s[:, 3:4], ln_s[:, 2:3], 1.0 / DM)
                ve.tensor_mul(ln_s[:, 4:5], ln_s[:, 1:2], ln_s[:, 1:2])
                ve.drain()
                sVE.inc(ve.tensor_sub(ln_s[:, 5:6], ln_s[:, 3:4],
                                      ln_s[:, 4:5]))
            var_rec = sVE.n
            var_recs.append(var_rec)

            @on("scalar")
            def _(sc, var_rec=var_rec):
                sc.wait_ge(sVE.sem, var_rec)
                sSC.inc(sc.activation(out=ln_s[:, 6:7], in_=ln_s[:, 5:6],
                                      func=AF.Sqrt, bias=EPS_LN))
            std_rec = sSC.n

            @on("vector")
            def _(ve, slot=slot, std_rec=std_rec, tt=tt):
                xt = xtile[:, slot * DM:(slot + 1) * DM]
                ve.wait_ge(sSC.sem, std_rec)
                if tt >= 1:
                    ve.wait_ge(trbU.sem, tr_last_rec[tt - 1])
                ve.drain()
                ve.reciprocal(ln_s[:, 7:8], ln_s[:, 6:7])
                ve.drain()
                sVE.inc(ve.tensor_scalar(
                    out=xnstage[:], in0=xt,
                    scalar1=ln_s[:, 1:2], scalar2=ln_s[:, 7:8],
                    op0=ALU.subtract, op1=ALU.mult))
            norm_rec.append(sVE.n)
            xn_rec = sVE.n

            for kc2 in range(KC):
                tbank = trb_slot[0] % 2
                trb_slot[0] += 1

                @on("tensor")
                def _(te, kc2=kc2, tbank=tbank, xn_rec=xn_rec):
                    if kc2 == 0:
                        te.wait_ge(sVE.sem, xn_rec)
                        if trbU.n == 0:
                            te.wait_ge(sPO.sem, p0_po)
                    if trbU.n >= 1:
                        te.wait_ge(trbF.sem, trbU.n - 1)
                    trbU.inc(te.transpose(
                        ptb[tbank][:, 0:128],
                        xnstage[:, kc2 * 128:(kc2 + 1) * 128], identb[:]))
                tr_rec = trbU.n

                @on("scalar")
                def _(sc, kc2=kc2, tbank=tbank, tt=tt, tt0=tt0,
                      tr_rec=tr_rec):
                    sc.wait_ge(trbU.sem, tr_rec)
                    if (not to_sbuf) and kc2 == 0 and tt0 >= 1:
                        sc.wait_ge(sStXnTo.sem, xnto_st_rec[tt0 - 1])
                    if to_sbuf:
                        dst = xnT[:, kc2 * L + tt * 128:
                                  kc2 * L + (tt + 1) * 128]
                    else:
                        dst = xnstage2[:, kc2 * 128:(kc2 + 1) * 128]
                    trbF.inc(sc.activation(
                        out=dst, in_=ptb[tbank][:, 0:128],
                        func=AF.Copy))
            tr_last_rec.append(trbU.n)
            ev_rec = trbF.n

            if not to_sbuf:
                @on("sync")
                def _(sy, tt0=tt0, ev_rec=ev_rec):
                    sy.wait_ge(trbF.sem, ev_rec)
                    view = xnstage2[:].rearrange("p (k c) -> p k c", k=KC)
                    dstap = sc_xnTo.ap().transpose([1, 0, 2])[
                        :, :, tt0 * 128:(tt0 + 1) * 128]
                    sStXnTo.inc(sy.dma_start(out=dstap, in_=view), 16)
                xnto_st_rec.append(sStXnTo.n)
        return trbF.n

    p1_trbF = emit_ln(d_xb, L // 128, True)
    emit_ln(d_xtok, TOK // 128, False)
    p1b_st = sStXnTo.n

    if phase_limit < 2:
        emit_finish()
        w1es.close()
        return nc

    @on("vector")
    def _(ve):
        sVE.inc(ve.memset(xbld[:, 0:DCV - 1], 0.0))

    mc_te_rec = []
    silu_rec = {}
    xst_rec = {}
    zst_list = []
    zsil_rec = []
    for Mc in range(32):
        wslot = Mc % 2

        @on("gpsimd")
        def _(gp, Mc=Mc, wslot=wslot):
            if Mc >= 2:
                gp.wait_ge(sTE.sem, mc_te_rec[Mc - 2])
            sWin[wslot].inc(gp.dma_start(
                out=w_in_sb[:, wslot * KC * 128:(wslot + 1) * KC * 128]
                .rearrange("p (k m) -> p k m", k=KC),
                in_=d_win.ap()[Mc]), 16)
        w_rec = (wslot, sWin[wslot].n)

        for ntg in range(2):
            @on("tensor")
            def _(te, Mc=Mc, ntg=ntg, wslot=wslot, w_rec=w_rec):
                if ntg == 0:
                    te.wait_ge(sWin[w_rec[0]].sem, w_rec[1])
                    if Mc == 0:
                        te.wait_ge(trbF.sem, p1_trbF)
                for j in range(4):
                    if buse[j] >= 1:
                        te.wait_ge(bankF[j].sem, buse[j])
                for kc2 in range(KC):
                    for j in range(4):
                        nt = ntg * 4 + j
                        ins = te.matmul(
                            pb[j][:],
                            w_in_sb[:, (wslot * KC + kc2) * 128:
                                    (wslot * KC + kc2 + 1) * 128],
                            xnT[:, kc2 * L + nt * 512:
                                kc2 * L + (nt + 1) * 512],
                            start=(kc2 == 0), stop=(kc2 == KC - 1))
                        if kc2 == KC - 1:
                            bankU[j].inc(ins)
                            buse[j] += 1
                            if ntg == 1 and j == 3:
                                sTE.inc(te.drain())
            for j in range(4):
                nt = ntg * 4 + j
                u_val = bankU[j].n

                if Mc < NDC:
                    dc = Mc

                    @on("vector")
                    def _(ve, j=j, nt=nt, u_val=u_val):
                        ve.wait_ge(bankU[j].sem, u_val)
                        ins = ve.tensor_copy(
                            xbld[:, DCV - 1 + nt * 512:
                                 DCV - 1 + (nt + 1) * 512],
                            pb[j][:])
                        bankF[j].inc(ins)
                        sVE.inc(ve.drain())
                else:
                    dc = Mc - NDC
                    zidx = len(zst_list)
                    zslot = zidx % 2

                    @on("scalar")
                    def _(sc, j=j, zslot=zslot, zidx=zidx, u_val=u_val):
                        if zidx >= 2:
                            sc.wait_ge(sStZ[zslot].sem, zst_list[zidx - 2])
                        sc.wait_ge(bankU[j].sem, u_val)
                        ins = sc.activation(
                            out=zsil[:, zslot * 512:(zslot + 1) * 512],
                            in_=pb[j][:], func=AF.Silu)
                        bankF[j].inc(ins)
                        sSC.inc(sc.drain())
                    zsil_rec.append(sSC.n)

                    @on("sync")
                    def _(sy, nt=nt, dc=dc, zslot=zslot, zidx=zidx):
                        sy.wait_ge(sSC.sem, zsil_rec[zidx])
                        sStZ[zslot].inc(sy.dma_start(
                            out=sc_sz.ap()[dc, :, nt * 512:(nt + 1) * 512],
                            in_=zsil[:, zslot * 512:(zslot + 1) * 512]), 16)
                    zst_list.append(sStZ[zslot].n)
        mc_te_rec.append(sTE.n)

        if Mc < NDC:
            dc = Mc
            evac_rec = sVE.n
            for nt in range(NT_L):
                ci = dc * NT_L + nt
                cslot = ci % 2
                xslot = ci % 2

                @on("vector")
                def _(ve, dc=dc, nt=nt, ci=ci, cslot=cslot):
                    if ci >= 2:
                        ve.wait_ge(sSC.sem, silu_rec[ci - 2])
                    ve.drain()
                    dst = cacc[:, cslot * 512:(cslot + 1) * 512]
                    ve.tensor_scalar(
                        out=dst, in0=xbld[:, nt * 512:nt * 512 + 512],
                        scalar1=cw[:, dc * DCV:dc * DCV + 1], scalar2=None,
                        op0=ALU.mult)
                    last = None
                    for jj in range(1, DCV):
                        ve.drain()
                        last = ve.scalar_tensor_tensor(
                            out=dst,
                            in0=xbld[:, nt * 512 + jj:nt * 512 + 512 + jj],
                            scalar=cw[:, dc * DCV + jj:dc * DCV + jj + 1],
                            in1=dst, op0=ALU.mult, op1=ALU.add)
                    sVE.inc(last)
                cv_rec = sVE.n

                @on("scalar")
                def _(sc, dc=dc, nt=nt, ci=ci, cslot=cslot, xslot=xslot,
                      cv_rec=cv_rec):
                    sc.wait_ge(sVE.sem, cv_rec)
                    if ci >= 2:
                        sc.wait_ge(sStXl[xslot].sem, xst_rec[ci - 2])
                    sSC.inc(sc.activation(
                        out=xldst[:, xslot * 512:(xslot + 1) * 512],
                        in_=cacc[:, cslot * 512:(cslot + 1) * 512],
                        func=AF.Silu, bias=cb[:, dc:dc + 1]))
                silu_rec[ci] = sSC.n

                @on("sync")
                def _(sy, dc=dc, nt=nt, ci=ci, xslot=xslot):
                    sy.wait_ge(sSC.sem, silu_rec[ci])
                    sStXl[xslot].inc(sy.dma_start(
                        out=sc_xld.ap()[dc, :, nt * 512:(nt + 1) * 512],
                        in_=xldst[:, xslot * 512:(xslot + 1) * 512]), 16)
                xst_rec[ci] = sStXl[xslot].n

    p2_xl = (sStXl[0].n, sStXl[1].n)
    p2_z = (sStZ[0].n, sStZ[1].n)
    w1es.close()
    barrier([(sStZ[0], p2_z[0]), (sStZ[1], p2_z[1]),
             (sStXl[0], p2_xl[0]), (sStXl[1], p2_xl[1]),
             (sStXnTo, p1b_st)])

    if phase_limit < 3:
        emit_finish()
        return nc

    w2es, w2 = mkwin([
        ("xp_sb", [128, NDC * 160], BF16),
        ("xldb", [128, 2 * NDC * 512], BF16),
        ("xdbl_s", [128, 2 * 512], F32),
        ("xdbl_s2", [32, 2 * 512], F32),
        ("dt_sb", [128, 16 * 128], BF16),
        ("dtlo", [128, 2 * 512], BF16),
        ("spl", [128, 8 * 512], F32),
        ("delst", [128, 2 * 512], F32),
    ])
    xp_sb, xldb = w2["xp_sb"], w2["xldb"]
    xdbl_s, xdbl_s2 = w2["xdbl_s"], w2["xdbl_s2"]
    dt_sb, dtlo, spl, delst = w2["dt_sb"], w2["dtlo"], w2["spl"], w2["delst"]

    @on("gpsimd")
    def _(gp):
        sLdXp.inc(gp.dma_start(
            out=xp_sb[:].rearrange("p (k m) -> p k m", k=NDC),
            in_=d_wxp.ap()), 16)
        sLdXp.inc(gp.dma_start(
            out=dt_sb[:].rearrange("p (k m) -> p k m", k=16),
            in_=d_wdt.ap().transpose([1, 0, 2])), 16)
    xp_w = sLdXp.n

    p3_te_rec = []
    p3_st_rec = []
    for nt in range(NT_L):
        rslot = nt % 2

        @on("gpsimd")
        def _(gp, nt=nt, rslot=rslot):
            if nt == 0:
                gp.wait_ge(sStXl[0].sem, p2_xl[0])
                gp.wait_ge(sStXl[1].sem, p2_xl[1])
            if nt >= 2:
                gp.wait_ge(sTE.sem, p3_te_rec[nt - 2])
            for dc in range(NDC):
                sLdXb[rslot].inc(gp.dma_start(
                    out=xldb[:, (rslot * NDC + dc) * 512:
                             (rslot * NDC + dc + 1) * 512],
                    in_=sc_xld.ap()[dc, :, nt * 512:(nt + 1) * 512]), 16)
        r_rec = (rslot, sLdXb[rslot].n)

        for mci in range(2):
            j = (nt * 2 + mci) % 4
            mwid = 128 if mci == 0 else 32

            @on("tensor")
            def _(te, nt=nt, mci=mci, j=j, mwid=mwid, rslot=rslot,
                  r_rec=r_rec):
                if mci == 0:
                    te.wait_ge(sLdXb[r_rec[0]].sem, r_rec[1])
                    if nt == 0:
                        te.wait_ge(sLdXp.sem, xp_w)
                if buse[j] >= 1:
                    te.wait_ge(bankF[j].sem, buse[j])
                for kc2 in range(NDC):
                    ins = te.matmul(
                        pb[j][0:mwid, :],
                        xp_sb[:, kc2 * 160 + mci * 128:
                              kc2 * 160 + mci * 128 + mwid],
                        xldb[:, (rslot * NDC + kc2) * 512:
                             (rslot * NDC + kc2 + 1) * 512],
                        start=(kc2 == 0), stop=(kc2 == NDC - 1))
                    if kc2 == NDC - 1:
                        bankU[j].inc(ins)
                        buse[j] += 1
                        if mci == 1:
                            sTE.inc(te.drain())
            u_val = bankU[j].n

            @on("vector")
            def _(ve, j=j, mci=mci, nt=nt, u_val=u_val, rslot=rslot):
                ve.wait_ge(bankU[j].sem, u_val)
                if nt >= 2:
                    ve.wait_ge(sStXd[rslot].sem, p3_st_rec[(nt - 2) * 2 + 1])
                if mci == 0:
                    ins = ve.tensor_copy(
                        xdbl_s[:, rslot * 512:(rslot + 1) * 512], pb[j][:])
                else:
                    ins = ve.tensor_copy(
                        xdbl_s2[0:32, rslot * 512:(rslot + 1) * 512],
                        pb[j][0:32, :])
                bankF[j].inc(ins)
                sVE.inc(ve.drain())
            ev_rec = sVE.n

            @on("sync")
            def _(sy, mci=mci, nt=nt, ev_rec=ev_rec, rslot=rslot):
                sy.wait_ge(sVE.sem, ev_rec)
                if mci == 0:
                    sStXd[rslot].inc(sy.dma_start(
                        out=sc_xdbl.ap()[0:128, nt * 512:(nt + 1) * 512],
                        in_=xdbl_s[:, rslot * 512:(rslot + 1) * 512]), 16)
                else:
                    sStXd[rslot].inc(sy.dma_start(
                        out=sc_xdbl.ap()[128:160, nt * 512:(nt + 1) * 512],
                        in_=xdbl_s2[0:32, rslot * 512:(rslot + 1) * 512]), 16)
            p3_st_rec.append(sStXd[rslot].n)
        p3_te_rec.append(sTE.n)
    p3_st = (sStXd[0].n, sStXd[1].n)

    @on("gpsimd")
    def _(gp):
        gp.wait_ge(sStXd[0].sem, p3_st[0])
        gp.wait_ge(sStXd[1].sem, p3_st[1])
        sCC.inc(gp.collective_compute(
            "AllReduce", ALU.add, replica_groups=GROUPS,
            ins=[sc_xdbl.ap()], outs=[sc_xdblr.ap()]))
    cc1 = sCC.n

    if phase_limit < 4:
        emit_finish()
        w2es.close()
        return nc

    p4_te_rec = []
    p4_add_rec = {}
    p4_st_rec = {}
    for nt in range(NT_L):
        rslot = nt % 2

        @on("gpsimd")
        def _(gp, nt=nt, rslot=rslot):
            if nt == 0:
                gp.wait_ge(sCC.sem, cc1)
            if nt >= 2:
                gp.wait_ge(sTE.sem, p4_te_rec[nt - 2])
            sLdDt[rslot].inc(gp.dma_start(
                out=dtlo[:, rslot * 512:(rslot + 1) * 512],
                in_=sc_xdblr.ap()[0:128, nt * 512:(nt + 1) * 512]), 16)
        r_rec = (rslot, sLdDt[rslot].n)

        for mc in range(NDC):
            it = nt * NDC + mc
            j = it % 4
            pslot = it % 2

            @on("tensor")
            def _(te, nt=nt, mc=mc, j=j, rslot=rslot, r_rec=r_rec):
                if mc == 0:
                    te.wait_ge(sLdDt[r_rec[0]].sem, r_rec[1])
                if buse[j] >= 1:
                    te.wait_ge(bankF[j].sem, buse[j])
                ins = te.matmul(pb[j][:], dt_sb[:, mc * 128:(mc + 1) * 128],
                                dtlo[:, rslot * 512:(rslot + 1) * 512],
                                start=True, stop=True)
                bankU[j].inc(ins)
                buse[j] += 1
                if mc == NDC - 1:
                    sTE.inc(te.drain())
            u_val = bankU[j].n

            @on("scalar")
            def _(sc, j=j, mc=mc, it=it, pslot=pslot, u_val=u_val):
                sc.wait_ge(bankU[j].sem, u_val)
                if it >= 2:
                    sc.wait_ge(sVE.sem, p4_add_rec[it - 2])
                bias = dtb[:, mc:mc + 1]
                o = pslot * 2048
                sc.drain()
                sc.activation(out=spl[:, o:o + 512], in_=pb[j][:],
                              func=AF.Relu, bias=bias)
                ins = sc.activation(out=spl[:, o + 512:o + 1024],
                                    in_=pb[j][:], func=AF.Abs, bias=bias)
                bankF[j].inc(ins)
                sc.drain()
                sc.activation(out=spl[:, o + 1024:o + 1536],
                              in_=spl[:, o + 512:o + 1024],
                              func=AF.Exp, scale=-1.0)
                sc.drain()
                sSC.inc(sc.activation(out=spl[:, o + 1536:o + 2048],
                                      in_=spl[:, o + 1024:o + 1536],
                                      func=AF.Ln, bias=1.0))
            sp_rec = sSC.n

            @on("vector")
            def _(ve, sp_rec=sp_rec, pslot=pslot, it=it):
                ve.wait_ge(sSC.sem, sp_rec)
                if it >= 2:
                    ve.wait_ge(sStDl[pslot].sem, p4_st_rec[it - 2])
                o = pslot * 2048
                ins = ve.tensor_add(delst[:, pslot * 512:(pslot + 1) * 512],
                                    spl[:, o:o + 512],
                                    spl[:, o + 1536:o + 2048])
                sVE.inc(ins)
            p4_add_rec[it] = sVE.n

            @on("sync")
            def _(sy, nt=nt, mc=mc, it=it, pslot=pslot):
                sy.wait_ge(sVE.sem, p4_add_rec[it])
                sStDl[pslot].inc(sy.dma_start(
                    out=sc_delta.ap()[mc, :, nt * 512:(nt + 1) * 512],
                    in_=delst[:, pslot * 512:(pslot + 1) * 512]), 16)
            p4_st_rec[it] = sStDl[pslot].n
        p4_te_rec.append(sTE.n)
    p4_dl = (sStDl[0].n, sStDl[1].n)

    w2es.close()
    barrier([(sStDl[0], p4_dl[0]), (sStDl[1], p4_dl[1])])

    if phase_limit < 5:
        emit_finish()
        return nc

    w3es, w3 = mkwin([
        ("xnTo", [128, KC * TOK], BF16),
        ("rt_sb", [128, KC * 128], BF16),
        ("lg_s", [128, 512], F32),
        ("ltm", [128, 128], F32),
        ("emat", [128, 128], F32),
        ("maskt", [128, 128], mybir.dt.uint32),
        ("coef", [128, 128], F32),
        ("coefb", [128, 128], BF16),
        ("coT", [128, 512], BF16),
        ("wmix_s", [8, 512], F32),
        ("wg_sb", [128, 2 * KC * 128], BF16),
        ("wu_sb", [128, 2 * KC * 128], BF16),
        ("sg_t", [128, 2 * 512], BF16),
        ("su_t", [128, 2 * 512], BF16),
        ("bas_t", [128, 2 * 512], BF16),
        ("mm_t", [128, 2 * 512], BF16),
        ("wmixb", [128, 2 * TOK], BF16),
        ("bc_acc", [128, TOK], F32),
        ("bc_b16", [128, TOK], BF16),
        ("hid_t", [128, 2 * 512], BF16),
    ])
    xnTo, rt_sb, lg_s, ltm = w3["xnTo"], w3["rt_sb"], w3["lg_s"], w3["ltm"]
    emat, maskt, coef, coefb = w3["emat"], w3["maskt"], w3["coef"], w3["coefb"]
    coT, wmix_s = w3["coT"], w3["wmix_s"]
    wg_sb, wu_sb = w3["wg_sb"], w3["wu_sb"]
    sg_t, su_t, bas_t, mm_t = w3["sg_t"], w3["su_t"], w3["bas_t"], w3["mm_t"]
    wmixb, bc_acc, bc_b16, hid_t = (w3["wmixb"], w3["bc_acc"], w3["bc_b16"],
                                    w3["hid_t"])

    @on("sync")
    def _(sy):
        sy.wait_ge(sStXnTo.sem, p1b_st)
        for kc2 in range(KC):
            sLdXn.inc(sy.dma_start(out=xnTo[:, kc2 * TOK:(kc2 + 1) * TOK],
                                   in_=sc_xnTo.ap()[kc2]), 16)
    xnTo_ld = sLdXn.n

    @on("gpsimd")
    def _(gp):
        sLdRt.inc(gp.dma_start(
            out=rt_sb[:].rearrange("p (k m) -> p k m", k=KC),
            in_=d_wrt.ap()), 16)
    rt_w = sLdRt.n

    @on("vector")
    def _(ve):
        sVE.inc(ve.memset(bc_acc[:], 0.0))

    wmix_st_rec = []
    trf_cnt = [0]
    for nt in range(NT_F):
        jj = (nt * 2) % 4

        @on("tensor")
        def _(te, nt=nt):
            if nt == 0:
                te.wait_ge(sLdXn.sem, xnTo_ld)
                te.wait_ge(sLdRt.sem, rt_w)
            if buse[jj] >= 1:
                te.wait_ge(bankF[jj].sem, buse[jj])
            for kc2 in range(KC):
                ins = te.matmul(pb[jj][:], rt_sb[:, kc2 * 128:(kc2 + 1) * 128],
                                xnTo[:, kc2 * TOK + nt * 512:
                                     kc2 * TOK + (nt + 1) * 512],
                                start=(kc2 == 0), stop=(kc2 == KC - 1))
                if kc2 == KC - 1:
                    bankU[jj].inc(ins)
                    buse[jj] += 1
        u_val = bankU[jj].n

        @on("vector")
        def _(ve, u_val=u_val):
            ve.wait_ge(bankU[jj].sem, u_val)
            ins = ve.tensor_copy(lg_s[:], pb[jj][:])
            bankF[jj].inc(ins)
            sVE.inc(ve.drain())
        lg_rec = sVE.n

        for sub in range(4):
            fbank = trf_cnt[0] % 2
            trf_cnt[0] += 1

            @on("tensor")
            def _(te, sub=sub, lg_rec=lg_rec, fbank=fbank):
                if sub == 0:
                    te.wait_ge(sVE.sem, lg_rec)
                    if trfU.n == 0:
                        te.wait_ge(sPO.sem, p0_po)
                if trfU.n >= 1:
                    te.wait_ge(trfF.sem, trfU.n - 1)
                trfU.inc(te.transpose(ptf[fbank][:, 0:128],
                                      lg_s[:, sub * 128:(sub + 1) * 128],
                                      identf[:]))
            tr_rec = trfU.n

            @on("vector")
            def _(ve, sub=sub, tr_rec=tr_rec, fbank=fbank):
                ve.wait_ge(trfU.sem, tr_rec)
                ve.drain()
                ins = ve.tensor_copy(ltm[:], ptf[fbank][:, 0:128])
                trfF.inc(ins)
                ve.drain()
                ve.reduce_max(ln_s[:, 0:1], ltm[:], axis=AX.X)
                ve.drain()
                sVE.inc(ve.tensor_scalar_mul(ln_s[:, 1:2], ln_s[:, 0:1],
                                             -1.0))
            mx_rec = sVE.n

            @on("scalar")
            def _(sc, mx_rec=mx_rec):
                sc.wait_ge(sVE.sem, mx_rec)
                sSC.inc(sc.activation(out=emat[:], in_=ltm[:], func=AF.Exp,
                                      bias=ln_s[:, 1:2]))
            e_rec = sSC.n

            @on("vector")
            def _(ve, e_rec=e_rec, nt=nt, sub=sub):
                ve.wait_ge(sSC.sem, e_rec)
                if nt == 0 and sub == 0:
                    ve.wait_ge(sPO.sem, p0_po)
                ve.drain()
                ve.reduce_sum(ln_s[:, 2:3], emat[:], axis=AX.X)
                ve.max(out=tk[:, 0:8], in_=emat[:])
                ve.drain()
                ve.reduce_sum(ln_s[:, 3:4], tk[:, 0:4], axis=AX.X)
                ve.drain()
                ve.scalar_tensor_tensor(out=ln_s[:, 4:5], in0=ln_s[:, 2:3],
                                        scalar=EPS_MOE, in1=ln_s[:, 3:4],
                                        op0=ALU.mult, op1=ALU.add)
                ve.drain()
                ve.reciprocal(ln_s[:, 5:6], ln_s[:, 4:5])
                ve.tensor_scalar(out=maskt[:], in0=emat[:],
                                 scalar1=tk[:, 3:4], scalar2=None,
                                 op0=ALU.is_lt)
                ve.drain()
                ve.tensor_scalar_mul(coef[:], emat[:], ln_s[:, 5:6])
                ve.drain()
                ve.copy_predicated(coef[:], maskt[:], zz[:])
                ve.drain()
                sVE.inc(ve.tensor_copy(coefb[:], coef[:]))
            co_rec = sVE.n

            tbank5 = trb_slot[0] % 2
            trb_slot[0] += 1

            @on("tensor")
            def _(te, sub=sub, co_rec=co_rec, tbank5=tbank5):
                te.wait_ge(sVE.sem, co_rec)
                if trbU.n >= 1:
                    te.wait_ge(trbF.sem, trbU.n - 1)
                trbU.inc(te.transpose(
                    ptb[tbank5][:, 0:128], coefb[:], identb[:]))
            tb_rec = trbU.n

            @on("vector")
            def _(ve, sub=sub, tb_rec=tb_rec, tbank5=tbank5):
                ve.wait_ge(trbU.sem, tb_rec)
                ins = ve.tensor_copy(
                    coT[:, sub * 128:(sub + 1) * 128],
                    ptb[tbank5][:, 0:128])
                trbF.inc(ins)
                sVE.inc(ve.drain())
        coT_rec = sVE.n

        jw = (nt * 2 + 1) % 4

        @on("tensor")
        def _(te, nt=nt, coT_rec=coT_rec):
            te.wait_ge(sVE.sem, coT_rec)
            if nt == 0:
                te.wait_ge(sVE.sem, p0_ve)
            if buse[jw] >= 1:
                te.wait_ge(bankF[jw].sem, buse[jw])
            ins = te.matmul(pb[jw][0:8, :], mixes[:], coT[:],
                            start=True, stop=True)
            bankU[jw].inc(ins)
            buse[jw] += 1
        uw_val = bankU[jw].n

        @on("vector")
        def _(ve, uw_val=uw_val, nt=nt):
            ve.wait_ge(bankU[jw].sem, uw_val)
            if nt >= 1:
                ve.wait_ge(sStWm.sem, wmix_st_rec[nt - 1])
            ins = ve.tensor_copy(wmix_s[:], pb[jw][0:8, :])
            bankF[jw].inc(ins)
            sVE.inc(ve.drain())
        wm_rec = sVE.n

        @on("sync")
        def _(sy, nt=nt, wm_rec=wm_rec):
            sy.wait_ge(sVE.sem, wm_rec)
            sStWm.inc(sy.dma_start(
                out=sc_wmix.ap()[:, nt * 512:(nt + 1) * 512],
                in_=wmix_s[:]), 16)
        wmix_st_rec.append(sStWm.n)
    wmix_st = sStWm.n

    if phase_limit < 6:
        emit_finish()
        w3es.close()
        return nc

    gu_te_rec = []
    gu_ve_rec = []
    gu_nt_ve = []
    gu_sg_rec = {}
    bc_st_rec = []
    for ic in range(8):
        for k in range(8):
            Mc = k * 8 + ic
            it = ic * 8 + k
            wslot = it % 2

            @on("gpsimd")
            def _(gp, Mc=Mc, wslot=wslot, it=it, k=k):
                if it == 0:
                    gp.wait_ge(sStWm.sem, wmix_st)
                if it >= 2:
                    gp.wait_ge(sTE.sem, gu_te_rec[it - 2])
                    gp.wait_ge(sVE.sem, gu_ve_rec[it - 2])
                sLdGu[wslot].inc(gp.dma_start(
                    out=wg_sb[:, wslot * KC * 128:(wslot + 1) * KC * 128]
                    .rearrange("p (a m) -> p a m", a=KC),
                    in_=d_wg.ap()[Mc]), 16)
                sLdGu[wslot].inc(gp.dma_start(
                    out=wu_sb[:, wslot * KC * 128:(wslot + 1) * KC * 128]
                    .rearrange("p (a m) -> p a m", a=KC),
                    in_=d_wu.ap()[Mc]), 16)
                sLdGu[wslot].inc(gp.dma_start(
                    out=wmixb[:, wslot * TOK:(wslot + 1) * TOK],
                    in_=sc_wmix.ap()[k:k + 1, :].partition_broadcast(128)),
                    16)
            w_rec = (wslot, sLdGu[wslot].n)

            for nt in range(NT_F):
                jg = (nt % 2) * 2
                ju = jg + 1

                @on("tensor")
                def _(te, nt=nt, jg=jg, ju=ju, wslot=wslot, w_rec=w_rec):
                    if nt == 0:
                        te.wait_ge(sLdGu[w_rec[0]].sem, w_rec[1])
                    for j in (jg, ju):
                        if buse[j] >= 1:
                            te.wait_ge(bankF[j].sem, buse[j])
                    for kc2 in range(KC):
                        ins = te.matmul(
                            pb[jg][:],
                            wg_sb[:, (wslot * KC + kc2) * 128:
                                  (wslot * KC + kc2 + 1) * 128],
                            xnTo[:, kc2 * TOK + nt * 512:
                                 kc2 * TOK + (nt + 1) * 512],
                            start=(kc2 == 0), stop=(kc2 == KC - 1))
                        if kc2 == KC - 1:
                            bankU[jg].inc(ins)
                            buse[jg] += 1
                    for kc2 in range(KC):
                        ins = te.matmul(
                            pb[ju][:],
                            wu_sb[:, (wslot * KC + kc2) * 128:
                                  (wslot * KC + kc2 + 1) * 128],
                            xnTo[:, kc2 * TOK + nt * 512:
                                 kc2 * TOK + (nt + 1) * 512],
                            start=(kc2 == 0), stop=(kc2 == KC - 1))
                        if kc2 == KC - 1:
                            bankU[ju].inc(ins)
                            buse[ju] += 1
                            if nt == NT_F - 1:
                                sTE.inc(te.drain())
                ug, uu = bankU[jg].n, bankU[ju].n
                eslot = nt % 2

                @on("scalar")
                def _(sc, jg=jg, ug=ug, eslot=eslot, it=it, nt=nt):
                    sc.wait_ge(bankU[jg].sem, ug)
                    gi = it * NT_F + nt
                    if gi >= 2:
                        prev = (gi - 2)
                        sc.wait_ge(sVE.sem, gu_nt_ve[prev])
                    ins = sc.activation(
                        out=sg_t[:, eslot * 512:(eslot + 1) * 512],
                        in_=pb[jg][:], func=AF.Silu)
                    bankF[jg].inc(ins)
                    sSC.inc(sc.drain())
                gu_sg_rec[(it, nt)] = sSC.n
                sg_rec = sSC.n

                @on("vector")
                def _(ve, ju=ju, uu=uu, eslot=eslot, nt=nt, wslot=wslot,
                      sg_rec=sg_rec, it=it):
                    ve.wait_ge(bankU[ju].sem, uu)
                    ve.drain()
                    ins = ve.tensor_copy(
                        su_t[:, eslot * 512:(eslot + 1) * 512], pb[ju][:])
                    bankF[ju].inc(ins)
                    ve.wait_ge(sSC.sem, sg_rec)
                    ve.drain()
                    ve.tensor_mul(bas_t[:, eslot * 512:(eslot + 1) * 512],
                                  sg_t[:, eslot * 512:(eslot + 1) * 512],
                                  su_t[:, eslot * 512:(eslot + 1) * 512])
                    ve.drain()
                    ve.tensor_mul(mm_t[:, eslot * 512:(eslot + 1) * 512],
                                  bas_t[:, eslot * 512:(eslot + 1) * 512],
                                  wmixb[:, wslot * TOK + nt * 512:
                                        wslot * TOK + (nt + 1) * 512])
                    ve.drain()
                    ins = ve.tensor_add(
                        bc_acc[:, nt * 512:(nt + 1) * 512],
                        bc_acc[:, nt * 512:(nt + 1) * 512],
                        mm_t[:, eslot * 512:(eslot + 1) * 512])
                    sVE.inc(ins)
                gu_nt_ve.append(sVE.n)
            gu_te_rec.append(sTE.n)
            gu_ve_rec.append(sVE.n)

        @on("vector")
        def _(ve, ic=ic):
            if ic >= 1:
                ve.wait_ge(sStBc.sem, bc_st_rec[ic - 1])
            ve.drain()
            ve.tensor_copy(bc_b16[:], bc_acc[:])
            ve.drain()
            sVE.inc(ve.memset(bc_acc[:], 0.0))
        bcc_rec = sVE.n

        @on("sync")
        def _(sy, ic=ic, bcc_rec=bcc_rec):
            sy.wait_ge(sVE.sem, bcc_rec)
            sStBc.inc(sy.dma_start(out=sc_bc.ap()[ic], in_=bc_b16[:]), 16)
        bc_st_rec.append(sStBc.n)

    sh_te_rec = []
    sh_hid_rec = {}
    sh_st_rec = {}
    for Mc in range(16):
        wslot = Mc % 2

        @on("gpsimd")
        def _(gp, Mc=Mc, wslot=wslot):
            if Mc >= 2:
                gp.wait_ge(sTE.sem, sh_te_rec[Mc - 2])
            else:
                gp.wait_ge(sTE.sem, gu_te_rec[62 + Mc])
            sLdGu[wslot].inc(gp.dma_start(
                out=wg_sb[:, wslot * KC * 128:(wslot + 1) * KC * 128]
                .rearrange("p (a m) -> p a m", a=KC),
                in_=d_wsg.ap()[Mc]), 16)
            sLdGu[wslot].inc(gp.dma_start(
                out=wu_sb[:, wslot * KC * 128:(wslot + 1) * KC * 128]
                .rearrange("p (a m) -> p a m", a=KC),
                in_=d_wsu.ap()[Mc]), 16)
        w_rec = (wslot, sLdGu[wslot].n)

        for nt in range(NT_F):
            jg = (nt % 2) * 2
            ju = jg + 1
            gi = Mc * NT_F + nt

            @on("tensor")
            def _(te, nt=nt, jg=jg, ju=ju, wslot=wslot, w_rec=w_rec, Mc=Mc):
                if nt == 0:
                    te.wait_ge(sLdGu[w_rec[0]].sem, w_rec[1])
                    if Mc == 0:
                        te.wait_ge(sTE.sem, gu_te_rec[-1])
                for j in (jg, ju):
                    if buse[j] >= 1:
                        te.wait_ge(bankF[j].sem, buse[j])
                for kc2 in range(KC):
                    ins = te.matmul(
                        pb[jg][:],
                        wg_sb[:, (wslot * KC + kc2) * 128:
                              (wslot * KC + kc2 + 1) * 128],
                        xnTo[:, kc2 * TOK + nt * 512:
                             kc2 * TOK + (nt + 1) * 512],
                        start=(kc2 == 0), stop=(kc2 == KC - 1))
                    if kc2 == KC - 1:
                        bankU[jg].inc(ins)
                        buse[jg] += 1
                for kc2 in range(KC):
                    ins = te.matmul(
                        pb[ju][:],
                        wu_sb[:, (wslot * KC + kc2) * 128:
                              (wslot * KC + kc2 + 1) * 128],
                        xnTo[:, kc2 * TOK + nt * 512:
                             kc2 * TOK + (nt + 1) * 512],
                        start=(kc2 == 0), stop=(kc2 == KC - 1))
                    if kc2 == KC - 1:
                        bankU[ju].inc(ins)
                        buse[ju] += 1
                        if nt == NT_F - 1:
                            sTE.inc(te.drain())
            ug, uu = bankU[jg].n, bankU[ju].n
            eslot = nt % 2

            @on("scalar")
            def _(sc, jg=jg, ug=ug, eslot=eslot, Mc=Mc, nt=nt, gi=gi):
                sc.wait_ge(bankU[jg].sem, ug)
                if gi >= 2:
                    sc.wait_ge(sVE.sem, sh_hid_rec[gi - 2])
                else:
                    sc.wait_ge(sVE.sem, gu_ve_rec[-1])
                ins = sc.activation(
                    out=sg_t[:, eslot * 512:(eslot + 1) * 512],
                    in_=pb[jg][:], func=AF.Silu)
                bankF[jg].inc(ins)
                sSC.inc(sc.drain())
            sg_rec = sSC.n

            @on("vector")
            def _(ve, ju=ju, uu=uu, eslot=eslot, sg_rec=sg_rec, gi=gi):
                ve.wait_ge(bankU[ju].sem, uu)
                ve.drain()
                ins = ve.tensor_copy(
                    su_t[:, eslot * 512:(eslot + 1) * 512], pb[ju][:])
                bankF[ju].inc(ins)
                ve.wait_ge(sSC.sem, sg_rec)
                if gi >= 2:
                    ve.wait_ge(sStHd[eslot].sem, sh_st_rec[gi - 2])
                ve.drain()
                ins = ve.tensor_mul(
                    hid_t[:, eslot * 512:(eslot + 1) * 512],
                    sg_t[:, eslot * 512:(eslot + 1) * 512],
                    su_t[:, eslot * 512:(eslot + 1) * 512])
                sVE.inc(ins)
            sh_hid_rec[gi] = sVE.n

            @on("sync")
            def _(sy, Mc=Mc, nt=nt, gi=gi, eslot=eslot):
                sy.wait_ge(sVE.sem, sh_hid_rec[gi])
                sStHd[eslot].inc(sy.dma_start(
                    out=sc_hid.ap()[Mc, :, nt * 512:(nt + 1) * 512],
                    in_=hid_t[:, eslot * 512:(eslot + 1) * 512]), 16)
            sh_st_rec[gi] = sStHd[eslot].n
        sh_te_rec.append(sTE.n)
    p6_st_all = (sStBc.n, sStHd[0].n, sStHd[1].n)

    w3es.close()
    barrier([(sStBc, p6_st_all[0]), (sStHd[0], p6_st_all[1]),
             (sStHd[1], p6_st_all[2]), (sStWm, sStWm.n)])

    if phase_limit < 7:
        emit_finish()
        return nc

    w4es, w4 = mkwin([
        ("Bb", [128, DS * TW], BF16),
        ("Cb", [128, DS * TW], BF16),
        ("a_all", [128, 2 * DS * TW], F32),
        ("h_all", [128, DS * TW], F32),
        ("del_t", [128, 2 * TW], F32),
        ("xld_t", [128, 2 * TW], F32),
        ("sz_t", [128, 2 * TW], F32),
        ("u_t", [128, TW], F32),
        ("b_all", [128, DS * TW], F32),
        ("m_all", [128, DS * TW], BF16),
        ("y_t", [128, 2 * TW], F32),
        ("g_t", [128, 2 * TW], BF16),
    ])
    Bb, Cb, a_all, h_all = w4["Bb"], w4["Cb"], w4["a_all"], w4["h_all"]
    del_t, xld_t, sz_t = w4["del_t"], w4["xld_t"], w4["sz_t"]
    u_t, b_all, m_all = w4["u_t"], w4["b_all"], w4["m_all"]
    y_t, g_t = w4["y_t"], w4["g_t"]

    scan_rec = []
    ysc_rec = []
    epi_rec = {}
    gst_rec = []
    for tw in range(NTW):
        @on("gpsimd")
        def _(gp, tw=tw):
            if tw >= 1:
                gp.wait_ge(sVE.sem, scan_rec[tw * NDC - 1])
            for n in range(DS):
                sLdBC.inc(gp.dma_start(
                    out=Bb[:, n * TW:(n + 1) * TW],
                    in_=sc_xdblr.ap()[128 + n:129 + n, tw * TW:(tw + 1) * TW]
                    .partition_broadcast(128)), 16)
                sLdBC.inc(gp.dma_start(
                    out=Cb[:, n * TW:(n + 1) * TW],
                    in_=sc_xdblr.ap()[144 + n:145 + n, tw * TW:(tw + 1) * TW]
                    .partition_broadcast(128)), 16)
        bc_rec = sLdBC.n

        for dc in range(NDC):
            gidx = tw * NDC + dc
            slot = gidx % 2

            @on("sync")
            def _(sy, tw=tw, dc=dc, gidx=gidx, slot=slot):
                if gidx == 0:
                    sy.wait_ge(sStDl[0].sem, p4_dl[0])
                    sy.wait_ge(sStDl[1].sem, p4_dl[1])
                    sy.wait_ge(sStXl[0].sem, p2_xl[0])
                    sy.wait_ge(sStXl[1].sem, p2_xl[1])
                    sy.wait_ge(sStZ[0].sem, p2_z[0])
                    sy.wait_ge(sStZ[1].sem, p2_z[1])
                if gidx >= 2:
                    sy.wait_ge(sVE.sem, epi_rec[gidx - 2])
                sLdSc[slot].inc(sy.dma_start(
                    out=del_t[:, slot * TW:(slot + 1) * TW],
                    in_=sc_delta.ap()[dc, :, tw * TW:(tw + 1) * TW]), 16)
                sLdSc[slot].inc(sy.dma_start(
                    out=xld_t[:, slot * TW:(slot + 1) * TW],
                    in_=sc_xld.ap()[dc, :, tw * TW:(tw + 1) * TW]), 16)
                sLdSc[slot].inc(sy.dma_start(
                    out=sz_t[:, slot * TW:(slot + 1) * TW],
                    in_=sc_sz.ap()[dc, :, tw * TW:(tw + 1) * TW]), 16)
            ld_rec = (slot, sLdSc[slot].n)

            @on("scalar")
            def _(sc, dc=dc, gidx=gidx, slot=slot, ld_rec=ld_rec):
                sc.wait_ge(sLdSc[ld_rec[0]].sem, ld_rec[1])
                if gidx == 0:
                    sc.wait_ge(sVE.sem, p0_ve)
                if gidx >= 2:
                    sc.wait_ge(sVE.sem, scan_rec[gidx - 2])
                ao = slot * DS * TW
                ins = None
                for n in range(DS):
                    ins = sc.activation(
                        out=a_all[:, ao + n * TW:ao + (n + 1) * TW],
                        in_=del_t[:, slot * TW:(slot + 1) * TW],
                        func=AF.Exp,
                        scale=negA[:, dc * DS + n:dc * DS + n + 1])
                sSC.inc(ins)
            a_rec = sSC.n

            @on("vector")
            def _(ve, tw=tw, dc=dc, gidx=gidx, slot=slot, ld_rec=ld_rec,
                  a_rec=a_rec, bc_rec=bc_rec):
                ve.wait_ge(sLdSc[ld_rec[0]].sem, ld_rec[1])
                ve.wait_ge(sLdBC.sem, bc_rec)
                ve.wait_ge(sSC.sem, a_rec)
                if gidx >= 1:
                    ve.wait_ge(sPO.sem, ysc_rec[gidx - 1])
                dslc = del_t[:, slot * TW:(slot + 1) * TW]
                xslc = xld_t[:, slot * TW:(slot + 1) * TW]
                ao = slot * DS * TW
                ve.drain()
                ve.tensor_mul(u_t[:], dslc, xslc)
                ve.drain()
                for n in range(DS):
                    ve.tensor_mul(b_all[:, n * TW:(n + 1) * TW], u_t[:],
                                  Bb[:, n * TW:(n + 1) * TW])
                ve.drain()
                for n in range(DS):
                    ve.tensor_tensor_scan(
                        out=h_all[:, n * TW:(n + 1) * TW],
                        data0=a_all[:, ao + n * TW:ao + (n + 1) * TW],
                        data1=b_all[:, n * TW:(n + 1) * TW],
                        initial=state[:, dc * DS + n:dc * DS + n + 1],
                        op0=ALU.mult, op1=ALU.add)
                ve.drain()
                for n in range(DS):
                    ve.tensor_mul(m_all[:, n * TW:(n + 1) * TW],
                                  h_all[:, n * TW:(n + 1) * TW],
                                  Cb[:, n * TW:(n + 1) * TW])
                hv = h_all[:].rearrange("p (n t) -> p n t", n=DS)
                ve.tensor_copy(state[:, dc * DS:(dc + 1) * DS],
                               hv[:, :, TW - 1])
                sVE.inc(ve.drain())
            scan_rec.append(sVE.n)

            @on("gpsimd")
            def _(gp, gidx=gidx, slot=slot):
                gp.wait_ge(sVE.sem, scan_rec[gidx])
                yv = y_t[:, slot * TW:(slot + 1) * TW]
                gp.tensor_add(yv, m_all[:, 0:TW], m_all[:, TW:2 * TW])
                for n in range(2, DS):
                    gp.drain()
                    gp.tensor_add(yv, yv, m_all[:, n * TW:(n + 1) * TW])
                sPO.inc(gp.drain())
            ysc_rec.append(sPO.n)

            def emit_epi(j):
                jslot = j % 2
                jdc = j % NDC

                @on("vector")
                def _(ve, j=j, jslot=jslot, jdc=jdc):
                    ve.wait_ge(sPO.sem, ysc_rec[j])
                    if j >= 2:
                        ve.wait_ge(sStG[jslot].sem, gst_rec[j - 2])
                    yv = y_t[:, jslot * TW:(jslot + 1) * TW]
                    ve.drain()
                    ve.scalar_tensor_tensor(
                        out=yv, in0=xld_t[:, jslot * TW:(jslot + 1) * TW],
                        scalar=dvec[:, jdc:jdc + 1], in1=yv,
                        op0=ALU.mult, op1=ALU.add)
                    ve.drain()
                    sVE.inc(ve.tensor_mul(
                        g_t[:, jslot * TW:(jslot + 1) * TW], yv,
                        sz_t[:, jslot * TW:(jslot + 1) * TW]))
                epi_rec[j] = sVE.n

                @on("sync")
                def _(sy, j=j, jslot=jslot):
                    sy.wait_ge(sVE.sem, epi_rec[j])
                    jtw, sdc = j // NDC, j % NDC
                    sStG[jslot].inc(sy.dma_start(
                        out=sc_g.ap()[sdc * 128:(sdc + 1) * 128,
                                      jtw * TW:(jtw + 1) * TW],
                        in_=g_t[:, jslot * TW:(jslot + 1) * TW]), 16)
                gst_rec.append(sStG[jslot].n)

            if gidx >= 1:
                emit_epi(gidx - 1)
            if gidx == NTW * NDC - 1:
                emit_epi(gidx)
    p7_st = (sStG[0].n, sStG[1].n)

    if phase_limit < 8:
        emit_finish()
        w4es.close()
        return nc

    @on("gpsimd")
    def _(gp):
        gp.wait_ge(sStG[0].sem, p7_st[0])
        gp.wait_ge(sStG[1].sem, p7_st[1])
        sCC.inc(gp.collective_compute(
            "AllGather", ALU.bypass, replica_groups=GROUPS,
            ins=[sc_g.ap()], outs=[sc_gx.ap()]))
    cc2 = sCC.n

    w4es.close()
    barrier()

    if phase_limit < 9:
        emit_finish()
        return nc

    w5es, w5 = mkwin([
        ("gkc", [128, 32 * 512], BF16),
        ("gtmp", [128, 4 * 512], BF16),
        ("bckc", [128, 8 * 512], BF16),
        ("hidkc", [128, 16 * 512], BF16),
        ("wf_sb", [128, 2 * 56 * 128], BF16),
        ("sfin", [128, 2 * 512], BF16),
        ("xfin", [128, 4 * DM], F32),
        ("ostage", [128, 4 * DM], F32),
    ])
    gkc, gtmp, bckc, hidkc = w5["gkc"], w5["gtmp"], w5["bckc"], w5["hidkc"]
    wf_sb, sfin, xfin, ostage = (w5["wf_sb"], w5["sfin"], w5["xfin"],
                                 w5["ostage"])

    fin_te_rec = []
    fin_ve_rec = []
    sfin_tr_rec = {}
    gcomb_rec = {}
    ost_rec = []
    for nt in range(NT_F):
        for kc2 in range(32):
            pslot = kc2 % 2

            @on("sync")
            def _(sy, nt=nt, kc2=kc2, pslot=pslot):
                if nt == 0 and kc2 == 0:
                    sy.wait_ge(sCC.sem, cc2)
                if kc2 >= 2:
                    sy.wait_ge(sVE.sem, gcomb_rec[(nt, kc2 - 2)])
                elif kc2 < 2 and nt >= 1:
                    sy.wait_ge(sVE.sem, gcomb_rec[(nt - 1, 30 + kc2)])
                r0 = kc2 * 128
                sLdG[pslot].inc(sy.dma_start(
                    out=gtmp[:, pslot * 1024:pslot * 1024 + 512],
                    in_=sc_gx.ap()[r0:r0 + 128, nt * 512:(nt + 1) * 512]),
                    16)
                sLdG[pslot].inc(sy.dma_start(
                    out=gtmp[:, pslot * 1024 + 512:pslot * 1024 + 1024],
                    in_=sc_gx.ap()[r0:r0 + 128,
                                   TOK + nt * 512:TOK + (nt + 1) * 512]), 16)
            gld_rec = (pslot, sLdG[pslot].n)

            @on("vector")
            def _(ve, nt=nt, kc2=kc2, pslot=pslot, gld_rec=gld_rec):
                ve.wait_ge(sLdG[gld_rec[0]].sem, gld_rec[1])
                if nt >= 1 and kc2 == 0:
                    ve.wait_ge(sTE.sem, fin_te_rec[nt * 16 - 1])
                ve.drain()
                ve.tensor_scalar_mul(
                    gkc[:, kc2 * 512:(kc2 + 1) * 512],
                    gtmp[:, pslot * 1024:pslot * 1024 + 512], m0[:, 0:1])
                ve.drain()
                ins = ve.scalar_tensor_tensor(
                    out=gkc[:, kc2 * 512:(kc2 + 1) * 512],
                    in0=gtmp[:, pslot * 1024 + 512:pslot * 1024 + 1024],
                    scalar=m1[:, 0:1],
                    in1=gkc[:, kc2 * 512:(kc2 + 1) * 512],
                    op0=ALU.mult, op1=ALU.add)
                sVE.inc(ins)
            gcomb_rec[(nt, kc2)] = sVE.n

        @on("sync")
        def _(sy, nt=nt):
            if nt == 0:
                sy.wait_ge(sStBc.sem, p6_st_all[0])
                sy.wait_ge(sStHd[0].sem, p6_st_all[1])
                sy.wait_ge(sStHd[1].sem, p6_st_all[2])
            if nt >= 1:
                sy.wait_ge(sTE.sem, fin_te_rec[nt * 16 - 1])
                sy.wait_ge(sVE.sem, fin_ve_rec[nt - 1])
            for kk in range(8):
                sLdFin.inc(sy.dma_start(
                    out=bckc[:, kk * 512:(kk + 1) * 512],
                    in_=sc_bc.ap()[kk, :, nt * 512:(nt + 1) * 512]), 16)
            for kk in range(16):
                sLdFin.inc(sy.dma_start(
                    out=hidkc[:, kk * 512:(kk + 1) * 512],
                    in_=sc_hid.ap()[kk, :, nt * 512:(nt + 1) * 512]), 16)
            for sub in range(4):
                sLdFin.inc(sy.dma_start(
                    out=xfin[:, sub * DM:(sub + 1) * DM],
                    in_=d_xtok.ap()[nt * 512 + sub * 128:
                                    nt * 512 + (sub + 1) * 128, :]), 16)
        rhs_rec = sLdFin.n

        for mc in range(16):
            it = nt * 16 + mc
            wslot = it % 2
            j = it % 4

            @on("gpsimd")
            def _(gp, mc=mc, wslot=wslot, it=it):
                if it >= 2:
                    gp.wait_ge(sTE.sem, fin_te_rec[it - 2])
                base = wslot * 56 * 128
                sLdWf[wslot].inc(gp.dma_start(
                    out=wf_sb[:, base:base + 32 * 128]
                    .rearrange("p (a m) -> p a m", a=32),
                    in_=d_wo.ap()[mc]), 16)
                sLdWf[wslot].inc(gp.dma_start(
                    out=wf_sb[:, base + 32 * 128:base + 40 * 128]
                    .rearrange("p (a m) -> p a m", a=8),
                    in_=d_wdn.ap()[mc]), 16)
                sLdWf[wslot].inc(gp.dma_start(
                    out=wf_sb[:, base + 40 * 128:base + 56 * 128]
                    .rearrange("p (a m) -> p a m", a=KC),
                    in_=d_wsd.ap()[mc]), 16)
            wf_rec = (wslot, sLdWf[wslot].n)

            @on("tensor")
            def _(te, nt=nt, mc=mc, j=j, wslot=wslot, wf_rec=wf_rec,
                  rhs_rec=rhs_rec, it=it):
                te.wait_ge(sLdWf[wf_rec[0]].sem, wf_rec[1])
                if mc == 0:
                    te.wait_ge(sVE.sem, gcomb_rec[(nt, 31)])
                    te.wait_ge(sLdFin.sem, rhs_rec)
                if buse[j] >= 1:
                    te.wait_ge(bankF[j].sem, buse[j])
                base = wslot * 56 * 128
                for kc2 in range(32):
                    te.matmul(pb[j][:],
                              wf_sb[:, base + kc2 * 128:
                                    base + (kc2 + 1) * 128],
                              gkc[:, kc2 * 512:(kc2 + 1) * 512],
                              start=(kc2 == 0), stop=False)
                for kk in range(8):
                    te.matmul(pb[j][:],
                              wf_sb[:, base + (32 + kk) * 128:
                                    base + (33 + kk) * 128],
                              bckc[:, kk * 512:(kk + 1) * 512],
                              start=False, stop=False)
                for kk in range(16):
                    ins = te.matmul(pb[j][:],
                                    wf_sb[:, base + (40 + kk) * 128:
                                          base + (41 + kk) * 128],
                                    hidkc[:, kk * 512:(kk + 1) * 512],
                                    start=False, stop=(kk == 15))
                    if kk == 15:
                        bankU[j].inc(ins)
                        buse[j] += 1
                        sTE.inc(te.drain())
            fin_te_rec.append(sTE.n)
            u_val = bankU[j].n
            eslot = it % 2

            @on("scalar")
            def _(sc, j=j, u_val=u_val, eslot=eslot, it=it):
                sc.wait_ge(bankU[j].sem, u_val)
                if it >= 2:
                    sc.wait_ge(trbU.sem, sfin_tr_rec[it - 2])
                ins = sc.activation(
                    out=sfin[:, eslot * 512:(eslot + 1) * 512],
                    in_=pb[j][:], func=AF.Copy)
                bankF[j].inc(ins)
                sSC.inc(sc.drain())
            sf_rec = sSC.n

            for sub in range(4):
                tbank9 = trb_slot[0] % 2
                trb_slot[0] += 1

                @on("tensor")
                def _(te, sub=sub, eslot=eslot, sf_rec=sf_rec, tbank9=tbank9):
                    if sub == 0:
                        te.wait_ge(sSC.sem, sf_rec)
                    if trbU.n >= 1:
                        te.wait_ge(trbF.sem, trbU.n - 1)
                    trbU.inc(te.transpose(
                        ptb[tbank9][:, 0:128],
                        sfin[:, eslot * 512 + sub * 128:
                             eslot * 512 + (sub + 1) * 128], identb[:]))
                tb_rec = trbU.n

                @on("vector")
                def _(ve, mc=mc, sub=sub, tb_rec=tb_rec, nt=nt,
                      tbank9=tbank9):
                    ve.wait_ge(trbU.sem, tb_rec)
                    if nt >= 1 and mc == 0 and sub == 0:
                        ve.wait_ge(sStOut.sem, ost_rec[nt - 1])
                    ins = ve.tensor_add(
                        ostage[:, sub * DM + mc * 128:
                               sub * DM + (mc + 1) * 128],
                        ptb[tbank9][:, 0:128],
                        xfin[:, sub * DM + mc * 128:sub * DM + (mc + 1) * 128])
                    trbF.inc(ins)
                    sVE.inc(ve.drain())
            sfin_tr_rec[it] = trbU.n
        fin_ve_rec.append(sVE.n)

        @on("sync")
        def _(sy, nt=nt):
            sy.wait_ge(sVE.sem, fin_ve_rec[nt])
            for sub in range(4):
                sStOut.inc(sy.dma_start(
                    out=d_out.ap()[nt * 512 + sub * 128:
                                   nt * 512 + (sub + 1) * 128, :],
                    in_=ostage[:, sub * DM:(sub + 1) * DM]), 16)
        ost_rec.append(sStOut.n)

    @on("sync")
    def _(sy):
        sy.wait_ge(sStOut.sem, sStOut.n)

    blk_cm.__exit__(None, None, None)
    w5es.close()
    return nc


def _stage(inputs, c):
    b, h = c // 2, c % 2
    f = np.ascontiguousarray

    def tile_k_m(w):
        Kd, Md = w.shape
        return f(w.reshape(Kd // 128, 128, Md // 128, 128)
                 .transpose(2, 1, 0, 3))

    x = inputs["L0_attn_hidden_states"]
    ipw = inputs["L0_attn_in_proj_w"]
    w_sel = np.concatenate([ipw[h * DH:(h + 1) * DH],
                            ipw[DI + h * DH:DI + (h + 1) * DH]], 0)
    win = tile_k_m(f(w_sel.T))
    wxp = f(inputs["L0_attn_x_proj_w"][:, h * DH:(h + 1) * DH].T
            .reshape(NDC, 128, 160).transpose(1, 0, 2))
    wdt = f(inputs["L0_attn_dt_proj_w"][h * DH:(h + 1) * DH].T
            .reshape(128, 16, 128).transpose(1, 0, 2))
    wrt = f(inputs["L0_ffn_w_router"].reshape(KC, 128, 128))
    wrt = f(wrt.transpose(1, 0, 2))
    wg = tile_k_m(inputs["L0_ffn_w_gate"])
    wu = tile_k_m(inputs["L0_ffn_w_up"])
    wsg = tile_k_m(inputs["L0_ffn_shared_w_gate"])
    wsu = tile_k_m(inputs["L0_ffn_shared_w_up"])
    wsd = tile_k_m(inputs["L0_ffn_shared_w_down"])
    wo = tile_k_m(f(inputs["L0_attn_out_proj_w"].T))
    wdn = tile_k_m(inputs["L0_ffn_w_down"])

    def vec_tile(v, width):
        return f(v.reshape(NDC, 128, width).transpose(1, 0, 2)
                 .reshape(128, NDC * width))

    alog = vec_tile(inputs["L0_attn_A_log"][h * DH:(h + 1) * DH], DS)
    cwv = vec_tile(inputs["L0_attn_conv_w"][h * DH:(h + 1) * DH], DCV)
    cbv = vec_tile(inputs["L0_attn_conv_b"][h * DH:(h + 1) * DH, None], 1)
    dtbv = vec_tile(inputs["L0_attn_dt_proj_b"][h * DH:(h + 1) * DH, None], 1)
    dvv = vec_tile(inputs["L0_attn_D"][h * DH:(h + 1) * DH, None], 1)
    amp = f(inputs["L0_ffn_amplitudes"].reshape(128, 16))
    es = f(inputs["L0_ffn_expert_scale"].reshape(128, 1))
    m0v = np.full((128, 1), 1.0 - h, np.float32)
    m1v = np.full((128, 1), float(h), np.float32)

    return dict(
        xb=f(x[b]), xtok=f(x[b, h * TOK:(h + 1) * TOK]),
        win=win, wxp=wxp, wdt=wdt, wrt=wrt, wg=wg, wu=wu, wsg=wsg, wsu=wsu,
        wsd=wsd, wo=wo, wdn=wdn, alog=alog, cw=cwv, cb=cbv, dtb=dtbv,
        dvec=dvv, amp=amp, es=es, m0=m0v, m1=m1v,
    )


def kernel(**inputs):
    inputs = {k: np.asarray(v) for k, v in inputs.items()}
    nc = build_nc()
    in_maps = [_stage(inputs, c) for c in range(8)]
    res = run_bass_kernel_spmd(nc, in_maps, core_ids=list(range(8)))
    out = np.empty((B, L, DM), np.float32)
    for c in range(8):
        b, h = c // 2, c % 2
        out[b, h * TOK:(h + 1) * TOK] = res.results[c]["out"]
    return out
